# version 16
# baseline (speedup 1.0000x reference)
import numpy as np
import ml_dtypes
from contextlib import ExitStack
import concourse.bass as bass
import concourse.mybir as mybir
from concourse.bass_utils import run_bass_kernel_spmd

F32 = mybir.dt.float32
BF16 = mybir.dt.bfloat16
AF = mybir.ActivationFunctionType
ALU = mybir.AluOpType
AX = mybir.AxisListType

T = 2048
C = 1024
NC = 8
HID = 4096
NH0, HD0 = 8, 128
NH1, HD1 = 16, 64
RD = 16
SCALE0 = 1.0 / float(np.sqrt(HD0))
SCALE1 = 1.0 / float(np.sqrt(HD1))
TF = 1024
NTT = 4
LN_EPS = 1e-5

ENGINES = ("sync", "gpsimd", "vector", "scalar", "tensor")


class Pl:

    def __init__(self, nc, es):
        self.nc = nc
        self.es = es
        self.q = {e: [] for e in ENGINES}
        self.sems = {}
        self.val = {}
        self.last = {}

    def sem(self, n):
        if n not in self.sems:
            self.sems[n] = self.es.enter_context(self.nc.semaphore(n))
            self.val[n] = 0
        return self.sems[n]

    def v(self, n):
        self.sem(n)
        return self.val[n]

    def wait(self, eng, n, v):
        s = self.sem(n)
        if v is None:
            self.q[eng].append(lambda e, s=s, n=n: e.wait_ge(s, self.val[n]))
        else:
            if eng in ("vector", "scalar") and self.AUTOSELF[eng] == n \
                    and v <= self.val.get(n, 0):
                return
            self.q[eng].append(lambda e, s=s, v=v: e.wait_ge(s, v))

    AUTOSELF = {"vector": "V", "scalar": "A", "gpsimd": "G"}

    def emit(self, eng, fn, inc=None, n=1):
        es_ = self.AUTOSELF.get(eng)
        if es_ is not None and self.val.get(es_, 0) > 0 and inc != "G0NOP":
            sw = self.sem(es_)
            vw = self.val[es_]
            self.q[eng].append(lambda e, s=sw, v=vw: e.wait_ge(s, v))
        if inc is None:
            self.q[eng].append(fn)
            return None
        s = self.sem(inc)
        self.val[inc] += n
        tgt = self.val[inc]
        self.q[eng].append(lambda e, fn=fn, s=s, n=n: fn(e).then_inc(s, n))
        return tgt

    ESEM = {"vector": "V", "scalar": "A", "gpsimd": "G", "tensor": "T"}

    def selfwait(self, eng):
        self.wait(eng, self.ESEM[eng], self.v(self.ESEM[eng]))

    def guard(self, eng, res):
        t = self.last.get(res)
        if t:
            self.wait(eng, t[0], t[1])

    def setlast(self, res, sem, val):
        self.last[res] = (sem, val)

    def finalize(self):
        with self.nc.Block() as block:
            @block.sync
            def _(e):
                for f in self.q["sync"]:
                    f(e)

            @block.gpsimd
            def _(e):
                for f in self.q["gpsimd"]:
                    f(e)

            @block.vector
            def _(e):
                for f in self.q["vector"]:
                    f(e)

            @block.scalar
            def _(e):
                for f in self.q["scalar"]:
                    f(e)

            @block.tensor
            def _(e):
                for f in self.q["tensor"]:
                    f(e)



def _bf(x):
    return np.ascontiguousarray(np.asarray(x), dtype=ml_dtypes.bfloat16)


def host_prep(inputs):
    f = np.float32
    h = {}
    x = np.asarray(inputs["L0_attn_x"], dtype=f)
    h["x_percore"] = [np.ascontiguousarray(x[b].T) for b in range(8)]

    h["wqkv0"] = np.ascontiguousarray(inputs["L0_attn_w_qkv"], dtype=f)
    h["wout0"] = np.ascontiguousarray(inputs["L0_attn_w_out"], dtype=f)
    h["hs0"] = np.asarray(inputs["L0_attn_head_scales"], dtype=f).reshape(1, 8)

    h["w1T"] = np.ascontiguousarray(np.asarray(inputs["L0_ffn_w1_w"], dtype=f).T)
    h["w2T"] = np.ascontiguousarray(np.asarray(inputs["L0_ffn_w2_w"], dtype=f).T)
    h["h1T"] = np.ascontiguousarray(np.asarray(inputs["L0_ffn_h1_w"], dtype=f).T)
    h["h2T"] = np.ascontiguousarray(np.asarray(inputs["L0_ffn_h2_w"], dtype=f).T)
    h["w3T"] = np.ascontiguousarray(np.asarray(inputs["L0_ffn_w3_w"], dtype=f).T)
    h["s1c"] = np.ascontiguousarray(np.asarray(inputs["L0_ffn_w1_s"], dtype=f).reshape(32, 128).T)
    h["s2c"] = np.ascontiguousarray(np.asarray(inputs["L0_ffn_w2_s"], dtype=f).reshape(32, 128).T)
    h["sh1c"] = np.ascontiguousarray(np.asarray(inputs["L0_ffn_h1_s"], dtype=f).reshape(32, 128).T)
    h["sh2c"] = np.ascontiguousarray(np.asarray(inputs["L0_ffn_h2_s"], dtype=f).reshape(32, 128).T)
    h["s3c"] = np.ascontiguousarray(np.asarray(inputs["L0_ffn_w3_s"], dtype=f).reshape(8, 128).T)

    h["wqkv1"] = np.ascontiguousarray(np.asarray(inputs["L1_attn_Wqkv"], dtype=f).T)
    h["bqkv1c"] = np.ascontiguousarray(np.asarray(inputs["L1_attn_bqkv"], dtype=f).reshape(24, 128).T)
    h["bqkv1v"] = np.ascontiguousarray(np.asarray(inputs["L1_attn_bqkv"], dtype=f)[2048:].reshape(1, 1024))
    h["wd1"] = np.ascontiguousarray(np.asarray(inputs["L1_attn_Wd"], dtype=f).T)
    h["bd1c"] = np.ascontiguousarray(np.asarray(inputs["L1_attn_bd"], dtype=f).reshape(8, 128).T)
    h["wrT"] = np.ascontiguousarray(np.asarray(inputs["L1_ffn_W_r"], dtype=f).T)
    h["wkT"] = np.ascontiguousarray(np.asarray(inputs["L1_ffn_W_k"], dtype=f).T)
    h["wvT"] = np.ascontiguousarray(np.asarray(inputs["L1_ffn_W_v"], dtype=f).T)
    h["murc"] = np.ascontiguousarray(np.asarray(inputs["L1_ffn_mu_r"], dtype=f).reshape(8, 128).T)
    h["mukc"] = np.ascontiguousarray(np.asarray(inputs["L1_ffn_mu_k"], dtype=f).reshape(8, 128).T)

    dirs = np.array([[(i >> 2 & 1) * 2 - 1, (i >> 1 & 1) * 2 - 1, (i & 1) * 2 - 1]
                     for i in range(8)], dtype=f) / np.sqrt(3.0, dtype=f)
    D3 = np.zeros((3, 24), dtype=f)
    for hh in range(8):
        D3[:, 3 * hh:3 * hh + 3] = np.outer(dirs[hh], dirs[hh])
    h["D3"] = D3
    h["I3"] = np.eye(3, dtype=f)
    h["identb"] = _bf(np.eye(128))
    R = np.zeros((128, 128), dtype=f)
    for base in (0, 64):
        for j in range(8):
            R[base + j, base + 8 + j] = -1.0
            R[base + 8 + j, base + j] = 1.0
    h["RT"] = _bf(R.T)
    invf = (10000.0 ** (-np.arange(0, RD, 2, dtype=np.float64) / RD)).astype(np.float64)
    t = np.arange(T, dtype=np.float64)
    cosrow = np.ones((128, T), dtype=f)
    sinrow = np.zeros((128, T), dtype=f)
    for base in (0, 64):
        for r in range(16):
            th = t * invf[r % 8]
            cosrow[base + r] = np.cos(th).astype(f)
            sinrow[base + r] = np.sin(th).astype(f)
    h["cosT"] = _bf(cosrow)
    h["sinT"] = _bf(sinrow)
    return h


WEIGHT_DECLS = {
    "wqkv0": ([C, 3 * C], F32), "wout0": ([C, C], F32), "hs0": ([1, 8], F32),
    "w1T": ([C, HID], F32), "w2T": ([C, HID], F32), "h1T": ([HID, HID], F32),
    "h2T": ([HID, HID], F32), "w3T": ([HID, C], F32),
    "s1c": ([128, 32], F32), "s2c": ([128, 32], F32), "sh1c": ([128, 32], F32),
    "sh2c": ([128, 32], F32), "s3c": ([128, 8], F32),
    "wqkv1": ([C, 3 * C], F32), "bqkv1c": ([128, 24], F32), "bqkv1v": ([1, C], F32),
    "wd1": ([C, C], F32), "bd1c": ([128, 8], F32),
    "wrT": ([C, C], F32), "wkT": ([C, C], F32), "wvT": ([C, C], F32),
    "murc": ([128, 8], F32), "mukc": ([128, 8], F32),
    "D3": ([3, 24], F32), "I3": ([3, 3], F32), "identb": ([128, 128], BF16),
    "RT": ([128, 128], BF16), "cosT": ([128, T], BF16), "sinT": ([128, T], BF16),
}


class Ctx:
    def __init__(self, nc, es, weight_names, io_decls):
        self.nc = nc
        self.es = es
        self.p = Pl(nc, es)
        self.dram = {}
        for n in weight_names:
            shape, dt = WEIGHT_DECLS[n]
            self.dram[n] = nc.dram_tensor(n, shape, dt, kind="ExternalInput").ap()
        for n, (shape, dt, kind) in io_decls.items():
            self.dram[n] = nc.dram_tensor(n, shape, dt, kind=kind).ap()
        self.sb = {}
        self.ps = {}
        self.DSL = "DSL"
        self.PW = [self.psum(f"PW{i}", [128, 1024], F32) for i in range(3)]
        self.PS6 = self.psum("PS6", [128, 512], F32)
        self.PTB = self.psum("PTB", [128, 512], BF16)

    def bank(self, i):
        if i < 6:
            return self.PW[i // 2][:, 512 * (i % 2): 512 * (i % 2) + 512]
        assert i == 6
        return self.PS6[:]

    def sbuf(self, name, shape, dt):
        if name not in self.sb:
            self.sb[name] = self.es.enter_context(self.nc.sbuf_tensor(name, shape, dt))
        return self.sb[name]

    def psum(self, name, shape, dt=F32):
        if name not in self.ps:
            self.ps[name] = self.es.enter_context(self.nc.psum_tensor(name, shape, dt))
        return self.ps[name]

    def commons(self):
        self.sbuf("onesb", [128, 1], BF16)
        self.sbuf("onesf", [1, 128], F32)
        self.sbuf("smalls", [1, 64], F32)
        self.sbuf("colb", [128, 48], F32)
        self.sbuf("epsc", [1, 1], F32)
        gmemset(self.p, self.sb["epsc"][:], LN_EPS)
        gmemset(self.p, self.sb["onesb"][:], 1.0)
        g0 = self.p.emit("gpsimd", lambda e, a=self.sb["onesf"][:]: e.memset(a, 1.0), inc="G")
        for eng in ("sync", "vector", "scalar", "tensor"):
            self.p.wait(eng, "G", g0)
        return g0



def mm(p, out, lhsT, rhs, start=True, stop=True, inc="T", n=1):
    return p.emit("tensor",
                  lambda e, o=out, l=lhsT, r=rhs, st=start, sp=stop: e.matmul(o, l, r, start=st, stop=sp),
                  inc=inc, n=n)


def tpose(p, out, in_, ident, inc="T"):
    return p.emit("tensor", lambda e, o=out, i=in_, d=ident: e.transpose(o, i, d), inc=inc)


def act(p, out, in_, func, bias=0.0, scale=1.0, accum=None, inc="A"):
    return p.emit("scalar",
                  lambda e, o=out, i=in_, f=func, b=bias, s=scale, a=accum:
                  e.activation(o, i, f, bias=b, scale=s, accum_out=a),
                  inc=inc)


def acopy(p, out, in_, inc="A"):
    return p.emit("scalar", lambda e, o=out, i=in_: e.copy(o, i), inc=inc)


def acopy_scale(p, out, in_, scale):
    return act(p, out, in_, AF.Copy, scale=scale)


def vcopy(p, out, in_, inc="V"):
    return p.emit("vector", lambda e, o=out, i=in_: e.tensor_copy(o, i), inc=inc)


def vtt(p, out, a, b, op, inc="V"):
    return p.emit("vector", lambda e, o=out, x=a, y=b, q=op: e.tensor_tensor(o, x, y, q), inc=inc)


def vts(p, out, in_, s1, op, inc="V"):
    return p.emit("vector", lambda e, o=out, i=in_, s=s1, q=op: e.tensor_scalar(o, i, s, None, q), inc=inc)


def vstt(p, out, in0, s, in1, op0, op1, inc="V"):
    return p.emit("vector",
                  lambda e, o=out, i0=in0, sc=s, i1=in1, a=op0, b=op1: e.scalar_tensor_tensor(o, i0, sc, i1, a, b),
                  inc=inc)


def vred(p, out, in_, op, absval=None, inc="V"):
    return p.emit("vector",
                  lambda e, o=out, i=in_, q=op, av=absval: e.tensor_reduce(o, i, AX.X, q, apply_absolute_value=av),
                  inc=inc)


def vrecip(p, out, in_, inc="V"):
    return p.emit("vector", lambda e, o=out, i=in_: e.reciprocal(o, i), inc=inc)


def dma(p, eng, out, in_, inc=None, n=16):
    return p.emit(eng, lambda e, o=out, i=in_: e.dma_start(out=o, in_=i), inc=inc, n=n)


def gmemset(p, ap, val, inc="G"):
    return p.emit("gpsimd", lambda e, a=ap, v=val: e.memset(a, v), inc=inc)


def gaffine(p, out, in_, cmp, fill, base, pattern, cm, inc="G"):
    return p.emit("gpsimd",
                  lambda e, o=out, i=in_, c=cmp, f=fill, b=base, pa=pattern, m=cm:
                  e.affine_select(out=o, in_=i, compare_op=c, fill=f, base=b, pattern=pa, channel_multiplier=m),
                  inc=inc)


def wait_all(p, eng, vals):
    for (s, v) in vals:
        p.wait(eng, s, v)


def ch(tile, c, a=0, b=T, rows=slice(0, 128)):
    return tile[rows, c * T + a: c * T + b]



def emit_ln(ctx, src, xn, ready):
    p = ctx.p
    bank = ctx.bank
    onesb = ctx.sb["onesb"]
    onesf = ctx.sb["onesf"]
    sq = ctx.sb["sq"]
    rows = ctx.sb["rowbuf"]
    tmpf = ctx.sb["tmpf"]

    cast_m = [None] * NC
    sq_m = [None] * NC
    for c in range(NC):
        wait_all(p, "vector", ready[c])
        wait_all(p, "scalar", ready[c])
        p.guard("vector", ("xn", c))
        cast_m[c] = vcopy(p, ch(xn, c), ch(src, c))
        p.guard("scalar", ("sqf", c))
        sq_m[c] = act(p, ch(sq, c), ch(src, c), AF.Square)
    for tt in range(NTT):
        for (bi, tile, res) in ((0, xn, 0), (1, sq, 2 * T)):
            p.guard("tensor", ("bank", bi))
            tm = None
            for c in range(NC):
                if tt == 0:
                    p.wait("tensor", "V" if bi == 0 else "A",
                           cast_m[c] if bi == 0 else sq_m[c])
                tm = mm(p, bank(bi)[0:1, :], onesb[:, 0:1], ch(tile, c, tt * 512, tt * 512 + 512),
                        start=(c == 0), stop=(c == NC - 1))
            p.wait("scalar", "T", tm)
            scale = (-1.0 / C) if bi == 0 else (1.0 / C)
            ev = acopy_scale(p, rows[0:1, res + tt * 512: res + tt * 512 + 512], bank(bi)[0:1, :], scale)
            p.setlast(("bank", bi), "A", ev)
    for c in range(NC):
        p.setlast(("sqf", c), "T", p.v("T"))
    p.selfwait("scalar")
    m2 = act(p, rows[0:1, 3 * T:4 * T], rows[0:1, 0:T], AF.Square)
    p.wait("vector", "A", m2)
    vsub = vtt(p, rows[0:1, 3 * T:4 * T], rows[0:1, 2 * T:3 * T], rows[0:1, 3 * T:4 * T], ALU.subtract)
    p.wait("scalar", "V", vsub)
    p.wait("scalar", "G", 0)
    act(p, rows[0:1, 3 * T:4 * T], rows[0:1, 3 * T:4 * T], AF.Ln, bias=ctx.sb["epsc"][0:1, 0:1])
    p.selfwait("scalar")
    rstd_m = act(p, rows[0:1, T:2 * T], rows[0:1, 3 * T:4 * T], AF.Exp, scale=-0.5)
    xn_ready = [None] * NC
    p.wait("tensor", "A", rstd_m)
    for tt in range(NTT):
        p.guard("tensor", ("bank", 4))
        p.guard("tensor", ("bank", 5))
        mm(p, bank(4), onesf[0:1, :], rows[0:1, tt * 512:tt * 512 + 512])
        b2 = mm(p, bank(5), onesf[0:1, :], rows[0:1, T + tt * 512:T + tt * 512 + 512])
        p.wait("vector", "T", b2)
        for c in range(NC):
            p.selfwait("vector")
            vtt(p, tmpf[:], ch(src, c, tt * 512, tt * 512 + 512), bank(4), ALU.add)
            p.selfwait("vector")
            xn_ready[c] = vtt(p, ch(xn, c, tt * 512, tt * 512 + 512), tmpf[:], bank(5), ALU.mult)
        p.setlast(("bank", 4), "V", xn_ready[NC - 1])
        p.setlast(("bank", 5), "V", xn_ready[NC - 1])
    for c in range(NC):
        p.setlast(("xn", c), "V", xn_ready[c])
    return [("V", xn_ready[c]) for c in range(NC)]


def emit_ln_stream(ctx, xn, sq, src_key):
    p = ctx.p
    bank = ctx.bank
    onesb = ctx.sb["onesb"]
    onesf = ctx.sb["onesf"]
    rows = ctx.sb["rowbuf"]
    tmpf = ctx.sb["tmpf"]
    xtmp = ctx.sb["xtmp"]
    srcd = ctx.dram[src_key]
    cast_ready = [None] * NC
    sq_ready = [None] * NC
    for c in range(NC):
        sl = c % 2
        p.guard("sync", ("xtD", sl))
        p.guard("sync", ("xtDv", sl))
        p.guard("sync", ("xtmp", sl))
        p.guard("sync", ("x1st", c))
        xl = dma(p, "sync", xtmp[:, sl * T:(sl + 1) * T], srcd[c * 128:(c + 1) * 128, :],
                 inc="DS%d" % sl)
        p.wait("vector", "DS%d" % sl, xl)
        p.guard("vector", ("xn", c))
        cast_ready[c] = vcopy(p, ch(xn, c), xtmp[:, sl * T:(sl + 1) * T])
        p.wait("scalar", "DS%d" % sl, xl)
        sq_ready[c] = act(p, ch(sq, c), xtmp[:, sl * T:(sl + 1) * T], AF.Square)
        p.setlast(("xtD", sl), "A", sq_ready[c])
        p.setlast(("xtDv", sl), "V", cast_ready[c])
    for c in range(NC):
        p.wait("tensor", "V", cast_ready[c])
        p.wait("tensor", "A", sq_ready[c])
    for tt in range(NTT):
        for (bi, tile, res) in ((0, xn, 0), (1, sq, 2 * T)):
            p.guard("tensor", ("bank", bi))
            tm = None
            for c in range(NC):
                tm = mm(p, bank(bi)[0:1, :], onesb[:, 0:1], ch(tile, c, tt * 512, tt * 512 + 512),
                        start=(c == 0), stop=(c == NC - 1))
            p.wait("scalar", "T", tm)
            scale = (-1.0 / C) if bi == 0 else (1.0 / C)
            ev = acopy_scale(p, rows[0:1, res + tt * 512: res + tt * 512 + 512], bank(bi)[0:1, :], scale)
            p.setlast(("bank", bi), "A", ev)
    p.selfwait("scalar")
    m2 = act(p, rows[0:1, 3 * T:4 * T], rows[0:1, 0:T], AF.Square)
    p.wait("vector", "A", m2)
    vsub = vtt(p, rows[0:1, 3 * T:4 * T], rows[0:1, 2 * T:3 * T], rows[0:1, 3 * T:4 * T], ALU.subtract)
    p.wait("scalar", "V", vsub)
    act(p, rows[0:1, 3 * T:4 * T], rows[0:1, 3 * T:4 * T], AF.Ln, bias=ctx.sb["epsc"][0:1, 0:1])
    p.selfwait("scalar")
    rstd_m = act(p, rows[0:1, T:2 * T], rows[0:1, 3 * T:4 * T], AF.Exp, scale=-0.5)
    xn_ready = [None] * NC
    p.wait("tensor", "A", rstd_m)
    for tt in range(NTT):
        p.guard("tensor", ("bank", 4))
        p.guard("tensor", ("bank", 5))
        mm(p, bank(4), onesf[0:1, :], rows[0:1, tt * 512:tt * 512 + 512])
        b2 = mm(p, bank(5), onesf[0:1, :], rows[0:1, T + tt * 512:T + tt * 512 + 512])
        p.wait("vector", "T", b2)
        for c in range(NC):
            p.selfwait("vector")
            vtt(p, tmpf[:], ch(xn, c, tt * 512, tt * 512 + 512), bank(4), ALU.add)
            p.selfwait("vector")
            xn_ready[c] = vtt(p, ch(xn, c, tt * 512, tt * 512 + 512), tmpf[:], bank(5), ALU.mult)
        p.setlast(("bank", 4), "V", xn_ready[NC - 1])
        p.setlast(("bank", 5), "V", xn_ready[NC - 1])
    for c in range(NC):
        p.setlast(("xn", c), "V", xn_ready[c])
    return [("V", xn_ready[c]) for c in range(NC)]



def emit_ffn0(ctx, xn, xn_ready):
    p = ctx.p
    bank = ctx.bank
    ha = ctx.sbuf("ha", [128, 32 * TF], BF16)
    hb = ctx.sbuf("hb", [128, 32 * TF], BF16)
    wb1 = ctx.sbuf("wb1", [128, 2 * 1024], BF16)
    wb2 = ctx.sbuf("wb2", [128, 2 * 1024], BF16)
    wbh = ctx.sbuf("wbh", [128, 2 * 4096], BF16)
    stmp = ctx.sbuf("stmp", [128, 2 * 512], BF16)
    outst = ctx.sbuf("outst", [128, 2 * TF], F32)
    s1c = ctx.sbuf("s1cs", [128, 32], F32)
    s2c = ctx.sbuf("s2cs", [128, 32], F32)
    sh1c = ctx.sbuf("sh1cs", [128, 32], F32)
    sh2c = ctx.sbuf("sh2cs", [128, 32], F32)
    s3c = ctx.sbuf("s3cs", [128, 8], F32)
    ffn0 = ctx.dram["ffn0"]
    for nm, tl in (("s1c", s1c), ("s2c", s2c), ("sh1c", sh1c), ("sh2c", sh2c), ("s3c", s3c)):
        dma(p, "sync", tl[:], ctx.dram[nm], inc=ctx.DSL)
    p.wait("scalar", ctx.DSL, None)

    w1T_r = ctx.dram["w1T"].rearrange("(cc p) m -> p cc m", p=128)
    w2T_r = ctx.dram["w2T"].rearrange("(cc p) m -> p cc m", p=128)
    h1T_r = ctx.dram["h1T"].rearrange("(kc p) m -> p kc m", p=128)
    h2T_r = ctx.dram["h2T"].rearrange("(kc p) m -> p kc m", p=128)
    w3T_r = ctx.dram["w3T"].rearrange("(kc p) m -> p kc m", p=128)

    wait_all(p, "tensor", xn_ready)
    for tf in range(T // TF):
        t0 = tf * TF
        for sl_ in range(2):
            p.setlast(("wb", sl_), "T", p.v("T"))
        for m in range(32):
            sl = m % 2
            p.guard("gpsimd", ("wb", sl))
            g1 = dma(p, "gpsimd", wb1[:, sl * 1024:(sl + 1) * 1024].rearrange("p (cc m) -> p cc m", m=128),
                     w1T_r[:, :, m * 128:(m + 1) * 128], inc="DGc%d" % sl)
            g = dma(p, "gpsimd", wb2[:, sl * 1024:(sl + 1) * 1024].rearrange("p (cc m) -> p cc m", m=128),
                    w2T_r[:, :, m * 128:(m + 1) * 128], inc="DG%d" % sl)
            p.wait("tensor", "DGc%d" % sl, g1)
            p.wait("tensor", "DG%d" % sl, g)
            tm = None
            for half in range(TF // 512):
                bs = 2 * (half % 2)
                p.guard("tensor", ("bank", bs))
                p.guard("tensor", ("bank", bs + 1))
                for c in range(NC):
                    mm(p, bank(bs), wb1[:, sl * 1024 + c * 128: sl * 1024 + c * 128 + 128],
                       ch(xn, c, t0 + half * 512, t0 + half * 512 + 512),
                       start=(c == 0), stop=(c == NC - 1))
                for c in range(NC):
                    tm = mm(p, bank(bs + 1), wb2[:, sl * 1024 + c * 128: sl * 1024 + c * 128 + 128],
                            ch(xn, c, t0 + half * 512, t0 + half * 512 + 512),
                            start=(c == 0), stop=(c == NC - 1))
                p.wait("scalar", "T", tm)
                p.guard("scalar", ("stmp", half))
                am = act(p, stmp[:, half * 512:(half + 1) * 512], bank(bs), AF.Silu, scale=s1c[:, m:m + 1])
                p.wait("vector", "A", am)
                vm = vstt(p, ha[:, m * TF + half * 512: m * TF + half * 512 + 512], bank(bs + 1),
                          s2c[:, m:m + 1], stmp[:, half * 512:(half + 1) * 512], ALU.mult, ALU.mult)
                p.setlast(("bank", bs), "V", vm)
                p.setlast(("bank", bs + 1), "V", vm)
                p.setlast(("stmp", half), "V", vm)
            p.setlast(("wb", sl), "T", tm)
        prev_sem, prev_val = "V", p.v("V")

        for (wr, sc, srct, dstt) in ((h1T_r, sh1c, ha, hb), (h2T_r, sh2c, hb, ha)):
            p.wait("tensor", prev_sem, prev_val)
            for sl_ in range(2):
                p.setlast(("wbh", sl_), "T", p.v("T"))
            for m2 in range(32):
                sl = m2 % 2
                p.guard("gpsimd", ("wbh", sl))
                g = dma(p, "gpsimd", wbh[:, sl * 4096:(sl + 1) * 4096].rearrange("p (kc m) -> p kc m", m=128),
                        wr[:, :, m2 * 128:(m2 + 1) * 128], inc="DG%d" % sl)
                p.wait("tensor", "DG%d" % sl, g)
                tm = None
                for half in range(TF // 512):
                    bi = (2 * m2 + half) % 4
                    p.guard("tensor", ("bank", bi))
                    for k in range(32):
                        tm = mm(p, bank(bi), wbh[:, sl * 4096 + k * 128: sl * 4096 + k * 128 + 128],
                                srct[:, k * TF + half * 512: k * TF + half * 512 + 512],
                                start=(k == 0), stop=(k == 31))
                    p.wait("scalar", "T", tm)
                    ev = act(p, dstt[:, m2 * TF + half * 512: m2 * TF + half * 512 + 512],
                             bank(bi), AF.Gelu, scale=sc[:, m2:m2 + 1])
                    p.setlast(("bank", bi), "A", ev)
                p.setlast(("wbh", sl), "T", tm)
            prev_sem, prev_val = "A", p.v("A")

        p.wait("tensor", "A", p.v("A"))
        for sl_ in range(2):
            p.setlast(("wbh", sl_), "T", p.v("T"))
        for mo in range(NC):
            sl = mo % 2
            p.guard("gpsimd", ("wbh", sl))
            g = dma(p, "gpsimd", wbh[:, sl * 4096:(sl + 1) * 4096].rearrange("p (kc m) -> p kc m", m=128),
                    w3T_r[:, :, mo * 128:(mo + 1) * 128], inc="DG%d" % sl)
            p.wait("tensor", "DG%d" % sl, g)
            tm = None
            for half in range(TF // 512):
                bi = (2 * mo + half) % 4
                p.guard("tensor", ("bank", bi))
                for k in range(32):
                    tm = mm(p, bank(bi), wbh[:, sl * 4096 + k * 128: sl * 4096 + k * 128 + 128],
                            ha[:, k * TF + half * 512: k * TF + half * 512 + 512],
                            start=(k == 0), stop=(k == 31))
                p.wait("scalar", "T", tm)
                p.guard("scalar", ("outst", (sl, half)))
                ev = act(p, outst[:, sl * TF + half * 512: sl * TF + half * 512 + 512],
                         bank(bi), AF.Copy, scale=s3c[:, mo:mo + 1])
                p.setlast(("bank", bi), "A", ev)
                p.wait("sync", "A", ev)
                sm = dma(p, "sync", ffn0[mo * 128:(mo + 1) * 128, t0 + half * 512: t0 + half * 512 + 512],
                         outst[:, sl * TF + half * 512: sl * TF + half * 512 + 512],
                         inc="DS%d_%d" % (sl, half))
                p.setlast(("outst", (sl, half)), "DS%d_%d" % (sl, half), sm)
            p.setlast(("wbh", sl), "T", tm)
    return p.v("DS0") + p.v("DS1")



def emit_attn0(ctx, xn, xn_ready, src_key="xT", dst_key="x1o"):
    p = ctx.p
    bank = ctx.bank
    q0 = ctx.sbuf("q0", [128, NC * T], BF16)
    k0 = ctx.sbuf("k0", [128, NC * T], BF16)
    vT0 = ctx.sbuf("vT0", [128, 16 * 1024], BF16)
    am0 = xn
    PT = ctx.sbuf("PT", [128, 16 * 512], BF16)
    wqb = ctx.sbuf("wqb", [128, 2 * 1024], BF16)
    wvsec = ctx.sbuf("wvsec", [128, NC * 512], BF16)
    rlb = ctx.sbuf("rlb", [128, 512], F32)
    hsrow = ctx.sbuf("hsrow", [1, 8], F32)
    d3t = ctx.sbuf("d3t", [3, 24], F32)
    i3t = ctx.sbuf("i3t", [3, 3], F32)
    b3 = ctx.sbuf("b3", [3, 24], BF16)
    c3 = ctx.sbuf("c3", [3, 8], F32)
    sq = PT
    rows = ctx.sb["rowbuf"]
    smalls = ctx.sb["smalls"]
    colb = ctx.sb["colb"]
    onesb = ctx.sb["onesb"]
    onesf = ctx.sb["onesf"]
    xtmp = ctx.sbuf("xtmp", [128, 2 * T], F32)
    xT = ctx.dram[src_key]
    x1o = ctx.dram[dst_key]

    wq_r = ctx.dram["wqkv0"].rearrange("(cc p) o -> p cc o", p=128)
    wo_r = ctx.dram["wout0"].rearrange("(cc p) o -> p cc o", p=128)

    dma(p, "sync", hsrow[:], ctx.dram["hs0"], inc=ctx.DSL)
    dma(p, "sync", d3t[:], ctx.dram["D3"], inc=ctx.DSL)
    dma(p, "sync", i3t[:], ctx.dram["I3"], inc=ctx.DSL)

    wait_all(p, "tensor", xn_ready)
    qk_done = [0] * 16
    for oc in range(16):
        sl = oc % 2
        p.guard("gpsimd", ("wqb", sl))
        g = dma(p, "gpsimd", wqb[:, sl * 1024:(sl + 1) * 1024].rearrange("p (cc o) -> p cc o", o=128),
                wq_r[:, :, oc * 128:(oc + 1) * 128], inc="DG%d" % sl)
        p.wait("tensor", "DG%d" % sl, g)
        tm = None
        for tt in range(NTT):
            bi = tt % 4
            p.guard("tensor", ("bank", bi))
            for c in range(NC):
                tm = mm(p, bank(bi), wqb[:, sl * 1024 + c * 128: sl * 1024 + c * 128 + 128],
                        ch(xn, c, tt * 512, tt * 512 + 512), start=(c == 0), stop=(c == NC - 1))
            p.wait("scalar", "T", tm)
            dst, cidx = (q0, oc) if oc < 8 else (k0, oc - 8)
            ev = act(p, ch(dst, cidx, tt * 512, tt * 512 + 512), bank(bi), AF.Copy,
                     scale=(SCALE0 if oc < 8 else 1.0))
            p.setlast(("bank", bi), "A", ev)
            qk_done[oc] = ev
        p.setlast(("wqb", sl), "T", tm)
    for dt in range(2):
        p.guard("gpsimd", ("wvsec",))
        wv = dma(p, "gpsimd", wvsec[:].rearrange("p (cc o) -> p cc o", o=512),
                 wq_r[:, :, 2048 + dt * 512: 2048 + dt * 512 + 512], inc="DGW")
        p.wait("tensor", "DGW", wv)
        tm = None
        for sb in range(16):
            bi = (sb * 2 + dt) % 4
            p.guard("tensor", ("bank", bi))
            for c in range(NC):
                tm = mm(p, bank(bi), ch(xn, c, sb * 128, sb * 128 + 128),
                        wvsec[:, c * 512: c * 512 + 512],
                        start=(c == 0), stop=(c == NC - 1))
            p.wait("scalar", "T", tm)
            ev = act(p, vT0[:, sb * 1024 + dt * 512: sb * 1024 + dt * 512 + 512], bank(bi), AF.Copy)
            p.setlast(("bank", bi), "A", ev)
        p.setlast(("wvsec",), "T", tm)

    p.wait("vector", ctx.DSL, None)
    p.wait("tensor", ctx.DSL, None)
    sqc = ctx.sbuf("sqc", [128, 2 * T], BF16)

    def setup_head(h):
        p.guard("tensor", ("bank", 4))
        cb = mm(p, bank(4)[0:3, 0:1], onesf[0:1, 0:3], hsrow[0:1, h:h + 1])
        p.wait("scalar", "T", cb)
        cm = acopy_scale(p, c3[:, h:h + 1], bank(4)[0:3, 0:1], float(np.sqrt(128.0)))
        p.setlast(("bank", 4), "A", cm)
        p.wait("vector", "A", cm)
        bm = vstt(p, b3[:, 3 * h:3 * h + 3], d3t[:, 3 * h:3 * h + 3], c3[:, h:h + 1],
                  i3t[:], ALU.mult, ALU.add)
        p.wait("tensor", "V", bm)
        p.wait("tensor", "A", qk_done[8 + h])
        ke = None
        for tt in range(NTT):
            p.guard("tensor", ("bank", 5))
            km = mm(p, bank(5)[0:3, :], b3[:, 3 * h:3 * h + 3],
                    ch(k0, h, tt * 512, tt * 512 + 512, rows=slice(0, 3)))
            p.wait("scalar", "T", km)
            ke = acopy(p, ch(k0, h, tt * 512, tt * 512 + 512, rows=slice(0, 3)), bank(5)[0:3, :])
            p.setlast(("bank", 5), "A", ke)
        p.guard("scalar", ("sqc", 0))
        act(p, sqc[:, 0:T], ch(q0, h), AF.Square)
        p.guard("scalar", ("sqc", 1))
        skm = act(p, sqc[:, T:2 * T], ch(k0, h), AF.Square)
        p.wait("tensor", "A", skm)
        r2 = mq = None
        for src_off, sm_i in ((0, 0), (T, 1)):
            rm = None
            for tt in range(NTT):
                bi = 4 + tt % 2
                p.guard("tensor", ("bank", bi))
                r2 = mm(p, bank(bi)[0:1, :], onesb[:, 0:1],
                        sqc[:, src_off + tt * 512:src_off + tt * 512 + 512])
                p.wait("scalar", "T", r2)
                p.guard("scalar", ("nrow", tt))
                rm = acopy(p, rows[0:1, tt * 512:tt * 512 + 512], bank(bi)[0:1, :])
                p.setlast(("bank", bi), "A", rm)
            p.wait("vector", "A", rm)
            mq = vred(p, smalls[0:1, sm_i:sm_i + 1], rows[0:1, 0:T], ALU.max)
            for tt in range(NTT):
                p.setlast(("nrow", tt), "V", mq)
        p.setlast(("sqc", 0), "T", r2)
        p.setlast(("sqc", 1), "T", r2)
        p.wait("scalar", "V", mq)
        sqr = act(p, smalls[0:1, 2:4], smalls[0:1, 0:2], AF.Sqrt)
        p.wait("vector", "A", sqr)
        vtt(p, smalls[0:1, 4:5], smalls[0:1, 2:3], smalls[0:1, 3:4], ALU.mult)
        p.selfwait("vector")
        ng = vts(p, smalls[0:1, 5:6], smalls[0:1, 4:5], -1.0, ALU.mult)
        p.wait("tensor", "V", ng)
        p.guard("tensor", ("bank", 4))
        nb = mm(p, bank(4)[:, 0:1], onesf[0:1, :], smalls[0:1, 5:6])
        p.wait("scalar", "T", nb)
        p.selfwait("scalar")
        negM = acopy(p, colb[:, h:h + 1], bank(4)[:, 0:1])
        p.setlast(("bank", 4), "A", negM)
        p.setlast(("negM0", h), "A", negM)

    def attn_head(h):
        for qt in range(NTT):
            nsb = 4 * qt + 4
            lsum = None
            pt_rdy = {}
            for sbp in range(0, nsb, 2):
                p.guard("tensor", ("bank", 4))
                p.guard("tensor", ("bank", 5))
                p.wait("tensor", "A", p.last[("negM0", h)][1])
                for d_ in range(2):
                    sb = sbp + d_
                    sm = mm(p, bank(4 + d_), k0[0:128, h * T + sb * 128: h * T + sb * 128 + 128],
                            ch(q0, h, qt * 512, qt * 512 + 512))
                p.wait("scalar", "T", sm)
                p.guard("scalar", ("PT", sbp))
                p.guard("scalar", ("PT", sbp + 1))
                ev = act(p, PT[:, sbp * 512:(sbp + 2) * 512], ctx.PW[2][:], AF.Exp,
                         bias=colb[:, h:h + 1])
                p.setlast(("bank", 4), "A", ev)
                p.setlast(("bank", 5), "A", ev)
                for d_ in range(2):
                    sb = sbp + d_
                    pt_rdy[sb] = ("A", ev)
                    if sb >= 4 * qt:
                        p.wait("gpsimd", "A", ev)
                        gm = gaffine(p, PT[:, sb * 512:(sb + 1) * 512],
                                     PT[:, sb * 512:(sb + 1) * 512], ALU.is_ge, 0.0,
                                     qt * 512 - sb * 128, [[1, 512]], -1)
                        pt_rdy[sb] = ("G", gm)
                for d_ in range(2):
                    sb = sbp + d_
                    p.wait("tensor", *pt_rdy[sb])
                    if sb == 0:
                        p.guard("tensor", ("bank", 6))
                    lsum = mm(p, bank(6)[0:1, :], onesb[:, 0:1], PT[:, sb * 512:(sb + 1) * 512],
                              start=(sb == 0), stop=(sb == nsb - 1))
            p.wait("vector", "T", lsum)
            rl = vrecip(p, rows[0:1, T:T + 512], bank(6)[0:1, :])
            p.setlast(("bank", 6), "V", rl)
            p.wait("tensor", "V", rl)
            p.guard("tensor", ("bank", 3))
            rb = mm(p, bank(3), onesf[0:1, :], rows[0:1, T:T + 512])
            p.wait("scalar", "T", rb)
            p.guard("scalar", ("rlb",))
            rlbm = acopy(p, rlb[:], bank(3))
            p.setlast(("bank", 3), "A", rlbm)
            pvb = qt % 2
            p.guard("tensor", ("bank", pvb))
            tm = None
            for sb in range(nsb):
                tm = mm(p, bank(pvb), vT0[:, sb * 1024 + h * 128: sb * 1024 + h * 128 + 128],
                        PT[:, sb * 512:(sb + 1) * 512], start=(sb == 0), stop=(sb == nsb - 1))
            for sb in range(nsb):
                p.setlast(("PT", sb), "T", tm)
            p.wait("vector", "T", tm)
            p.wait("vector", "A", rlbm)
            ev = vtt(p, ch(am0, h, qt * 512, qt * 512 + 512), bank(pvb), rlb[:], ALU.mult)
            p.setlast(("bank", pvb), "V", ev)
            p.setlast(("rlb",), "V", ev)

    for h in range(NH0):
        setup_head(h)
        attn_head(h)

    tmpd = ctx.sb["tmpd"] if "tmpd" in ctx.sb else ctx.sbuf("tmpd", [128, 2 * T], F32)
    ffn0d = ctx.dram["ffn0"]
    p.wait("tensor", "V", p.v("V"))
    p.guard("sync", ("wvsec",))
    p.setlast(("tmpd", 0), "T", p.v("T"))
    p.setlast(("tmpd", 1), "T", p.v("T"))
    cast_ready = [None] * NC
    sq_ready = [None] * NC
    for co in range(NC):
        sl = co % 2
        p.guard("gpsimd", ("wqb", sl))
        g = dma(p, "gpsimd", wqb[:, sl * 1024:(sl + 1) * 1024].rearrange("p (cc o) -> p cc o", o=128),
                wo_r[:, :, co * 128:(co + 1) * 128], inc="DG%d" % sl)
        p.wait("tensor", "DG%d" % sl, g)
        p.guard("sync", ("xtmp", sl))
        xl = dma(p, "sync", xtmp[:, sl * T:(sl + 1) * T], xT[co * 128:(co + 1) * 128, :],
                 inc="DS%d" % sl)
        p.guard("sync", ("tmpd", sl))
        fl = dma(p, "sync", tmpd[:, sl * T:(sl + 1) * T], ffn0d[co * 128:(co + 1) * 128, :],
                 inc="DF%d" % sl)
        p.wait("vector", "DS%d" % sl, xl)
        p.wait("vector", "DF%d" % sl, fl)
        tm = None
        for tt in range(NTT):
            bi = 4 + (tt % 2)
            p.guard("tensor", ("bank", bi))
            for c in range(NC):
                tm = mm(p, bank(bi), wqb[:, sl * 1024 + c * 128: sl * 1024 + c * 128 + 128],
                        ch(am0, c, tt * 512, tt * 512 + 512), start=(c == 0), stop=(c == NC - 1))
            p.wait("vector", "T", tm)
            sltt = slice(sl * T + tt * 512, sl * T + tt * 512 + 512)
            vtt(p, xtmp[:, sltt], xtmp[:, sltt], bank(bi), ALU.add)
            ev = vtt(p, xtmp[:, sltt], xtmp[:, sltt], tmpd[:, sltt], ALU.add)
            p.setlast(("bank", bi), "V", ev)
            p.setlast(("tmpd", sl), "V", ev)
        p.setlast(("wqb", sl), "T", tm)
        p.wait("sync", "V", ev)
        st = dma(p, "sync", x1o[co * 128:(co + 1) * 128, :], xtmp[:, sl * T:(sl + 1) * T],
                 inc="DT%d" % sl)
        p.setlast(("xtmp", sl), "DT%d" % sl, st)
        p.setlast(("x1st", co), "DT%d" % sl, st)
    return None



def emit_attn1(ctx, xn, xn_ready, src_key="x1i", dst_key="x1o"):
    p = ctx.p
    bank = ctx.bank
    q1 = ctx.sbuf("q1", [128, NC * T], BF16)
    k1 = ctx.sbuf("k1", [128, NC * T], BF16)
    vT1 = ctx.sbuf("vT1", [128, 16 * 1024], BF16)
    am1 = xn[:, 0:16 * 1024]
    amT = q1
    ET = ctx.sbuf("PT", [128, 16 * 512], BF16)
    Ab = ctx.sbuf("Ab", [128, 16 * 80], BF16)
    yb = ctx.sbuf("yb", [128, 2 * 64], BF16)
    dgt = ctx.sbuf("dgt", [128, 128], BF16)
    tmb = ctx.sbuf("tmb", [128, 2 * 512], BF16)
    wqb = ctx.sbuf("wqb", [128, 2 * 1024], BF16)
    wvsec = ctx.sbuf("wvsec", [128, NC * 512], BF16)
    idt = ctx.sbuf("idt", [128, 128], BF16)
    rt = ctx.sbuf("rt", [128, 128], BF16)
    cost = ET[:, 0:T]
    sint = ET[:, T:2 * T]
    bqc = ctx.sbuf("bqc", [128, 24], F32)
    bqs = ctx.sbuf("bqs", [128, 8], F32)
    bdc = ctx.sbuf("bdc", [128, 8], F32)
    bvb = ctx.sbuf("bvb", [1, C], BF16)
    orb = ctx.sbuf("orb", [1, 128], BF16)
    sq = am1
    rows = ctx.sb["rowbuf"]
    smalls = ctx.sb["smalls"]
    colb = ctx.sb["colb"]
    onesb = ctx.sb["onesb"]
    onesf = ctx.sb["onesf"]
    PTB = ctx.PTB

    wq_r = ctx.dram["wqkv1"].rearrange("(cc p) o -> p cc o", p=128)
    wd_r = ctx.dram["wd1"].rearrange("(cc p) o -> p cc o", p=128)

    dma(p, "sync", bqc[:], ctx.dram["bqkv1c"], inc=ctx.DSL)
    dma(p, "sync", bdc[:], ctx.dram["bd1c"], inc=ctx.DSL)
    bvf = ctx.sb["rowbuf"][0:1, 0:C]
    dma(p, "sync", bvf, ctx.dram["bqkv1v"], inc=ctx.DSL)
    dma(p, "sync", idt[:], ctx.dram["identb"], inc=ctx.DSL)
    dma(p, "sync", rt[:], ctx.dram["RT"], inc=ctx.DSL)
    dma(p, "sync", cost, ctx.dram["cosT"], inc=ctx.DSL)
    dma(p, "sync", sint, ctx.dram["sinT"], inc=ctx.DSL)
    gmemset(p, orb[:], 1.0)
    for i in range(16):
        gmemset(p, Ab[:, i * 80 + 64:i * 80 + 65], 1.0)
    ab_ones = p.emit("gpsimd", lambda e, a=Ab[0:1, 0:1]: e.nop(), inc="G")
    p.wait("vector", ctx.DSL, None)
    bq = vts(p, bqs[:], bqc[:, 0:8], SCALE1, ALU.mult)
    bv = vcopy(p, bvb[:], bvf)
    p.setlast(("nrow", 0), "V", bv)
    p.setlast(("nrow", 1), "V", bv)
    p.setlast(("wvsec",), "V", bv)

    wait_all(p, "tensor", xn_ready)
    p.wait("tensor", ctx.DSL, None)
    qk_done = [0] * 16
    for oc in range(16):
        sl = oc % 2
        p.guard("gpsimd", ("wqb", sl))
        g = dma(p, "gpsimd", wqb[:, sl * 1024:(sl + 1) * 1024].rearrange("p (cc o) -> p cc o", o=128),
                wq_r[:, :, oc * 128:(oc + 1) * 128], inc="DG%d" % sl)
        p.wait("tensor", "DG%d" % sl, g)
        tm = None
        for tt in range(NTT):
            bi = tt % 4
            p.guard("tensor", ("bank", bi))
            for c in range(NC):
                tm = mm(p, bank(bi), wqb[:, sl * 1024 + c * 128: sl * 1024 + c * 128 + 128],
                        ch(xn, c, tt * 512, tt * 512 + 512), start=(c == 0), stop=(c == NC - 1))
            p.wait("scalar", "T", tm)
            if oc < 8:
                p.wait("scalar", "V", bq)
                ev = act(p, ch(q1, oc, tt * 512, tt * 512 + 512), bank(bi), AF.Identity,
                         scale=SCALE1, bias=bqs[:, oc:oc + 1])
            else:
                ev = act(p, ch(k1, oc - 8, tt * 512, tt * 512 + 512), bank(bi), AF.Identity,
                         bias=bqc[:, oc:oc + 1])
            p.setlast(("bank", bi), "A", ev)
            qk_done[oc] = ev
        p.setlast(("wqb", sl), "T", tm)
    p.wait("tensor", "V", bv)
    p.wait("tensor", "G", ab_ones)
    for dt in range(2):
        p.guard("gpsimd", ("wvsec",))
        wv = dma(p, "gpsimd", wvsec[:].rearrange("p (cc o) -> p cc o", o=512),
                 wq_r[:, :, 2048 + dt * 512: 2048 + dt * 512 + 512], inc="DGW")
        p.wait("tensor", "DGW", wv)
        tm = None
        for sb in range(16):
            bi = (sb * 2 + dt) % 4
            p.guard("tensor", ("bank", bi))
            for c in range(NC):
                mm(p, bank(bi), ch(xn, c, sb * 128, sb * 128 + 128),
                   wvsec[:, c * 512: c * 512 + 512],
                   start=(c == 0), stop=False)
            tm = mm(p, bank(bi), orb[0:1, :], bvb[0:1, dt * 512:dt * 512 + 512],
                    start=False, stop=True)
            p.wait("scalar", "T", tm)
            ev = act(p, vT1[:, sb * 1024 + dt * 512: sb * 1024 + dt * 512 + 512], bank(bi), AF.Copy)
            p.setlast(("bank", bi), "A", ev)
            p.setlast(("vte", sb, dt), "A", ev)
        p.setlast(("wvsec",), "T", tm)
    vT1d = ctx.dram["vT1d"]
    for sb in range(16):
        p.wait("sync", *p.last[("vte", sb, 0)])
        p.wait("sync", *p.last[("vte", sb, 1)])
        dma(p, "sync", vT1d[sb * 128:(sb + 1) * 128, :],
            vT1[:, sb * 1024:(sb + 1) * 1024], inc="DVT")

    rope_done = [0] * 16
    for gi, gt in enumerate((q1, k1)):
        for cc in range(NC):
            p.wait("tensor", "A", qk_done[gi * 8 + cc])
            for tt in range(NTT):
                bi = 4 + (tt % 2)
                p.guard("tensor", ("bank", bi))
                pm = mm(p, bank(bi), rt[:], ch(gt, cc, tt * 512, tt * 512 + 512))
                p.wait("vector", "T", pm)
                p.guard("vector", ("tmb", 0))
                t1 = vtt(p, tmb[:, 0:512], ch(gt, cc, tt * 512, tt * 512 + 512),
                         cost[:, tt * 512:tt * 512 + 512], ALU.mult)
                p.guard("vector", ("tmb", 1))
                t2 = vtt(p, tmb[:, 512:1024], bank(bi), sint[:, tt * 512:tt * 512 + 512], ALU.mult)
                p.setlast(("bank", bi), "V", t2)
                ev = vtt(p, ch(gt, cc, tt * 512, tt * 512 + 512), tmb[:, 0:512], tmb[:, 512:1024],
                         ALU.add)
                p.setlast(("tmb", 0), "V", ev)
                p.setlast(("tmb", 1), "V", ev)
                rope_done[gi * 8 + cc] = ev

    for cc in range(NC):
        p.wait("scalar", "V", rope_done[cc])
        p.wait("scalar", "V", rope_done[8 + cc])
        act(p, sq[:, 0:T], ch(q1, cc), AF.Square)
        skm = act(p, sq[:, T:2 * T], ch(k1, cc), AF.Square)
        p.wait("tensor", "A", skm)
        for h2 in range(2):
            h = 2 * cc + h2
            r0 = 64 * h2
            for src_off, sm_i in ((0, 0), (T, 1)):
                rm = None
                for tt in range(NTT):
                    bi = 4 + tt % 2
                    p.guard("tensor", ("bank", bi))
                    r2 = mm(p, bank(bi)[0:1, :], onesb[r0:r0 + 64, 0:1],
                            sq[r0:r0 + 64, src_off + tt * 512:src_off + tt * 512 + 512])
                    p.wait("scalar", "T", r2)
                    p.guard("scalar", ("nrow", tt))
                    rm = acopy(p, rows[0:1, tt * 512:tt * 512 + 512], bank(bi)[0:1, :])
                    p.setlast(("bank", bi), "A", rm)
                p.wait("vector", "A", rm)
                mq = vred(p, smalls[0:1, sm_i:sm_i + 1], rows[0:1, 0:T], ALU.max)
                for tt in range(NTT):
                    p.setlast(("nrow", tt), "V", mq)
            p.wait("scalar", "V", mq)
            sqr = act(p, smalls[0:1, 2:4], smalls[0:1, 0:2], AF.Sqrt)
            p.wait("vector", "A", sqr)
            vtt(p, smalls[0:1, 4:5], smalls[0:1, 2:3], smalls[0:1, 3:4], ALU.mult)
            ng = vts(p, smalls[0:1, 5:6], smalls[0:1, 4:5], -1.0, ALU.mult)
            p.wait("tensor", "V", ng)
            p.guard("tensor", ("bank", 4))
            nb = mm(p, bank(4)[:, 0:1], onesf[0:1, :], smalls[0:1, 5:6])
            p.wait("scalar", "T", nb)
            negM = acopy(p, colb[:, h:h + 1], bank(4)[:, 0:1])
            p.setlast(("bank", 4), "A", negM)
            p.setlast(("negM", h), "A", negM)

    ET2 = ctx.sbuf("ET2", [128, 16 * 512], BF16)
    Ab2 = ctx.sbuf("Ab2", [128, 16 * 80], BF16)
    yb2 = ctx.sbuf("yb2", [128, 2 * 64], BF16)
    for i in range(16):
        gmemset(p, Ab2[:, i * 80 + 64:i * 80 + 65], 1.0)
    ab2_ones = p.emit("gpsimd", lambda e, a=Ab2[0:1, 0:1]: e.nop(), inc="G")
    p.wait("tensor", "G", ab2_ones)
    rope_all = max(rope_done)
    p.wait("scalar", "V", rope_all)
    CH = ctx.sbuf("CHst", [128, 40 * 128], BF16)
    vst = ctx.sbuf("vst", [128, 2 * 512], BF16)
    HS = ({"Ab": Ab, "yb": yb}, {"Ab": Ab2, "yb": yb2})

    def chsl(chain, slot):
        o = (slot * 8 + chain) * 128
        return CH[:, o:o + 128]

    def chgrp(grp, slot):
        o = (slot * 8 + grp * 4) * 128
        return CH[:, o:o + 512]

    def etsl(half, hi, a, b):
        if half == 0:
            t = ET if hi == 0 else ET2
            return t[:, a:b]
        o = hi * 8192
        return vT1[:, o + a: o + b]

    evict_waited = [False]

    def prep_chunks(hp, pn, half, out):
        nsb = 4 * pn + 4
        lst = []

        def exp_unit(sbp, hi):
            def emit():
                h = 2 * hp + hi
                r0 = 64 * hi
                p.guard("tensor", ("bank", 4))
                p.guard("tensor", ("bank", 5))
                p.wait("tensor", "A", p.last[("negM", h)][1])
                smm = None
                for d_ in range(2):
                    sb = sbp + d_
                    smm = mm(p, bank(4 + d_),
                             k1[r0:r0 + 64, hp * T + sb * 128: hp * T + sb * 128 + 128],
                             q1[r0:r0 + 64, hp * T + pn * 512: hp * T + pn * 512 + 512])
                p.wait("scalar", "T", smm)
                if half == 1 and not evict_waited[0]:
                    evict_waited[0] = True
                    p.wait("scalar", "DVT", None)
                for d_ in range(2):
                    p.guard("scalar", ("PT", half, hi, sbp + d_))
                    p.guard("scalar", ("PTv", half, hi, sbp + d_))
                ev = act(p, etsl(half, hi, sbp * 512, (sbp + 2) * 512), ctx.PW[2][:], AF.Exp,
                         bias=colb[:, h:h + 1])
                p.setlast(("bank", 4), "A", ev)
                p.setlast(("bank", 5), "A", ev)
                for d_ in range(2):
                    sb = sbp + d_
                    p.setlast(("ET", half, hi, sb), "A", ev)
                    if sb >= 4 * pn:
                        p.wait("gpsimd", "A", ev)
                        gm = gaffine(p, etsl(half, hi, sb * 512, (sb + 1) * 512),
                                     etsl(half, hi, sb * 512, (sb + 1) * 512), ALU.is_ge, 0.0,
                                     pn * 512 - sb * 128, [[1, 512]], -1)
                        p.setlast(("ET", half, hi, sb), "G", gm)
            return emit

        for sbp in range(0, nsb, 2):
            for hi in range(2):
                lst.append(exp_unit(sbp, hi))

        def l_unit(hi, ii):
            def emit():
                i = 4 * pn + ii
                col = hi * 4 + ii
                lb = col % 2
                p.guard("tensor", ("bank", lb))
                lm = None
                for j in range(i + 1):
                    p.wait("tensor", *p.last[("ET", half, hi, j)])
                    lm = mm(p, bank(lb)[:, 0:1],
                            etsl(half, hi, j * 512 + ii * 128, j * 512 + ii * 128 + 128),
                            onesb[:, 0:1], start=(j == 0), stop=(j == i))
                p.wait("vector", "T", lm)
                lc_ = 16 + half * 16 + col
                rc = vrecip(p, colb[:, lc_:lc_ + 1], bank(lb)[:, 0:1])
                p.setlast(("bank", lb), "V", rc)
                out["lrec"][(hi, ii)] = colb[:, lc_:lc_ + 1]
                p.guard("vector", ("dgt",))
                vtt(p, dgt[:], etsl(half, hi, i * 512 + ii * 128, i * 512 + ii * 128 + 128),
                    idt[:], ALU.mult)
                dc_ = 24 + half * 16 + col
                dg = vred(p, colb[:, dc_:dc_ + 1], dgt[:], ALU.add)
                p.setlast(("dgt",), "V", dg)
                out["dgcol"][(hi, ii)] = (colb[:, dc_:dc_ + 1], dg)
            return emit

        for hi in range(2):
            for ii in range(4):
                lst.append(l_unit(hi, ii))
        return lst

    def chains_panel(hp, pn, half, lrec):
        vstate = {}
        for hi in range(2):
            for ii in range(4):
                chn = hi * 4 + ii
                i = 4 * pn + ii
                p.guard("vector", ("ch", chn, 0))
                v0 = vts(p, chsl(chn, 0),
                         etsl(half, hi, i * 512 + ii * 128, i * 512 + ii * 128 + 128),
                         lrec[(hi, ii)], ALU.mult)
                p.wait("gpsimd", "V", v0)
                v0g = gaffine(p, chsl(chn, 0), chsl(chn, 0), ALU.is_ge, 0.0, -1, [[1, 128]], -1)
                vstate[chn] = {"v": ("G", v0g)}
        for grp in range(2):
            p.guard("tensor", ("PTB",))
            tp = None
            for lc in range(4):
                chn = grp * 4 + lc
                p.wait("tensor", *vstate[chn]["v"])
                tp = tpose(p, PTB[:, lc * 128:(lc + 1) * 128], chsl(chn, 0), idt[:])
            p.wait("scalar", "T", tp)
            for lc in range(4):
                p.guard("scalar", ("ch", grp * 4 + lc, 2))
            ycp = acopy(p, chgrp(grp, 2), PTB[:])
            p.setlast(("PTB",), "A", ycp)
            for lc in range(4):
                chn = grp * 4 + lc
                vstate[chn]["y"] = ("A", ycp)
                p.wait("vector", *vstate[chn]["v"])
                p.guard("vector", ("ch", chn, 4))
                x0 = vtt(p, chsl(chn, 4), chsl(chn, 0), idt[:], ALU.add)
                vstate[chn]["x"] = ("V", x0)
        for ml in range(1, 7):
            vc = (ml - 1) % 2
            vn = 1 - vc
            for grp in range(2):
                bv_, by_ = grp, 2 + grp
                p.guard("tensor", ("bank", bv_))
                p.guard("tensor", ("bank", by_))
                tmv = None
                for lc in range(4):
                    chn = grp * 4 + lc
                    s = vstate[chn]
                    p.wait("tensor", *s["v"])
                    p.wait("tensor", *s["y"])
                    mm(p, bank(bv_)[:, lc * 128:(lc + 1) * 128],
                       chsl(chn, 2 + vc), chsl(chn, vc))
                    tmv = mm(p, bank(by_)[:, lc * 128:(lc + 1) * 128],
                             chsl(chn, vc), chsl(chn, 2 + vc))
                eng = "vector" if grp else "scalar"
                esem = "V" if grp else "A"
                p.wait(eng, "T", tmv)
                p.guard(eng, ("chg", grp, vn))
                p.guard(eng, ("chg", grp, 2 + vn))
                if eng == "vector":
                    e1 = vcopy(p, chgrp(grp, vn), bank(bv_))
                    e2 = vcopy(p, chgrp(grp, 2 + vn), bank(by_))
                else:
                    e1 = acopy(p, chgrp(grp, vn), bank(bv_))
                    e2 = acopy(p, chgrp(grp, 2 + vn), bank(by_))
                p.setlast(("bank", bv_), esem, e2)
                p.setlast(("bank", by_), esem, e2)
                p.setlast(("chg", grp, vc), "T", tmv)
                p.setlast(("chg", grp, 2 + vc), "T", tmv)
                for lc in range(4):
                    chn = grp * 4 + lc
                    vstate[chn]["v"] = (esem, e1)
                    vstate[chn]["y"] = (esem, e2)
            for grp in range(2):
                bx = 4 + grp
                p.guard("tensor", ("bank", bx))
                tmx = None
                for lc in range(4):
                    chn = grp * 4 + lc
                    s = vstate[chn]
                    p.wait("tensor", *s["y"])
                    p.wait("tensor", *s["x"])
                    tmx = mm(p, bank(bx)[:, lc * 128:(lc + 1) * 128],
                             chsl(chn, 2 + vn), chsl(chn, 4))
                p.setlast(("chg", grp, 2 + vn), "T", tmx)
                p.wait("vector", "T", tmx)
                p.guard("vector", ("chg", grp, 4))
                xe = vtt(p, chgrp(grp, 4), bank(bx), chgrp(grp, 4), ALU.add)
                p.setlast(("bank", bx), "V", xe)
                for lc in range(4):
                    vstate[grp * 4 + lc]["x"] = ("V", xe)
        return vstate

    def seq_panel(hp, pn, half, lrec, dgcol, vstate, vsl, chunks):
        nblk = 0
        L = len(chunks)
        p.wait("vector", "DV%d" % vsl, p.last[("vstl", vsl)][1])
        for ii in range(4):
            i = 4 * pn + ii
            for hi in range(2):
                h = 2 * hp + hi
                st = HS[hi]
                chn = hi * 4 + ii
                pcb = 2 + hi
                p.guard("tensor", ("bank", pcb))
                lm = None
                for j in range(i):
                    p.wait("tensor", *p.last[("ET", half, hi, j)])
                    p.guard("tensor", ("Ab", hi, j))
                    lm = mm(p, bank(pcb)[:, 0:64],
                            etsl(half, hi, j * 512 + ii * 128, j * 512 + ii * 128 + 128),
                            st["Ab"][:, j * 80: j * 80 + 64], start=(j == 0), stop=(j == i - 1))
                dgc, dgm = dgcol[(hi, ii)]
                p.wait("vector", "V", dgm)
                vo = vsl * 512 + ii * 128 + hi * 64
                if i > 0:
                    p.wait("vector", "T", lm)
                    p.guard("vector", ("yb", hi, 0))
                    y0 = vstt(p, st["yb"][:, 0:64], vst[:, vo:vo + 64],
                              dgc, bank(pcb)[:, 0:64], ALU.mult, ALU.add)
                else:
                    p.guard("vector", ("yb", hi, 0))
                    y0 = vts(p, st["yb"][:, 0:64], vst[:, vo:vo + 64], dgc, ALU.mult)
                p.setlast(("bank", pcb), "V", y0)
                p.guard("tensor", ("bank", 6))
                p.wait("tensor", "V", y0)
                p.wait("tensor", *vstate[chn]["x"])
                bm_ = mm(p, bank(6)[:, 0:64], chsl(chn, 4), st["yb"][:, 0:64])
                p.wait("vector", "T", bm_)
                ai = vts(p, st["Ab"][:, i * 80: i * 80 + 64], bank(6)[:, 0:64],
                         lrec[(hi, ii)], ALU.mult)
                p.setlast(("bank", 6), "V", ai)
                p.setlast(("Ab", hi, i), "V", ai)
                p.setlast(("yb", hi, 0), "V", ai)
                p.wait("scalar", "V", ai)
                acopy(p, am1[:, i * 1024 + h * 64: i * 1024 + h * 64 + 64],
                      st["Ab"][:, i * 80: i * 80 + 64])
                for sl_ in range(5):
                    p.setlast(("ch", chn, sl_), "V", ai)
                    p.setlast(("chg", chn // 4, sl_), "V", ai)
                nblk += 1
                for c_ in chunks[(nblk - 1) * L // 8: nblk * L // 8]:
                    c_()
        for sb in range(4 * pn + 4):
            for hi in range(2):
                p.setlast(("PT", half, hi, sb), "T", p.v("T"))
                p.setlast(("PTv", half, hi, sb), "V", p.v("V"))
        p.setlast(("vst", vsl), "V", p.v("V"))

    def load_vst(hp, pn, vsl):
        p.guard("sync", ("vst", vsl))
        lv = None
        for ii in range(4):
            i = 4 * pn + ii
            lv = dma(p, "sync", vst[:, vsl * 512 + ii * 128: vsl * 512 + (ii + 1) * 128],
                     vT1d[i * 128:(i + 1) * 128, hp * 128:(hp + 1) * 128],
                     inc="DV%d" % vsl)
        p.setlast(("vstl", vsl), "DV%d" % vsl, lv)

    p.wait("sync", "DVT", None)
    units = [(hp, pn) for hp in range(NH1 // 2) for pn in range(NTT)]
    pend = None
    for ui, (hp, pn) in enumerate(units):
        half = ui % 2
        vsl = ui % 2
        load_vst(hp, pn, vsl)
        out = {"lrec": {}, "dgcol": {}}
        chunks = prep_chunks(hp, pn, half, out)
        if pend is None:
            for c_ in chunks:
                c_()
        else:
            seq_panel(*(pend + (chunks,)))
        vstate = chains_panel(hp, pn, half, out["lrec"])
        pend = (hp, pn, half, out["lrec"], out["dgcol"], vstate, vsl)
    seq_panel(*(pend + ([],)))
    p.setlast(("wvsec",), "T", p.v("T"))

    p.wait("tensor", "V", p.v("V"))
    p.wait("tensor", "A", p.v("A"))
    for i in range(16):
        for dc4 in range(NC // 4):
            p.guard("tensor", ("PTB",))
            tp = None
            for lc in range(4):
                dc = dc4 * 4 + lc
                tp = tpose(p, PTB[:, lc * 128:(lc + 1) * 128],
                           am1[:, i * 1024 + dc * 128: i * 1024 + dc * 128 + 128], idt[:])
            p.wait("vector", "T", tp)
            for lc in range(4):
                dc = dc4 * 4 + lc
                ev = vcopy(p, ch(amT, dc, i * 128, i * 128 + 128),
                           PTB[:, lc * 128:(lc + 1) * 128])
                p.setlast(("amT", dc), "V", ev)
            p.setlast(("PTB",), "V", ev)

    xtmp = ctx.sbuf("xtmp", [128, 2 * T], F32)
    x1i = ctx.dram[src_key]
    x1o = ctx.dram[dst_key]
    for dc in range(NC):
        p.wait("tensor", *p.last[("amT", dc)])
    p.guard("sync", ("wvsec",))
    p.wait("sync", "V", p.v("V"))
    for co in range(NC):
        sl = co % 2
        p.guard("gpsimd", ("wqb", sl))
        g = dma(p, "gpsimd", wqb[:, sl * 1024:(sl + 1) * 1024].rearrange("p (cc o) -> p cc o", o=128),
                wd_r[:, :, co * 128:(co + 1) * 128], inc="DG%d" % sl)
        p.wait("tensor", "DG%d" % sl, g)
        p.guard("sync", ("xtmp", sl))
        xl = dma(p, "sync", xtmp[:, sl * T:(sl + 1) * T], x1i[co * 128:(co + 1) * 128, :],
                 inc="DS%d" % sl)
        p.wait("vector", "DS%d" % sl, xl)
        tm = None
        for tt in range(NTT):
            bi = 4 + (tt % 2)
            p.guard("tensor", ("bank", bi))
            for c in range(NC):
                tm = mm(p, bank(bi), wqb[:, sl * 1024 + c * 128: sl * 1024 + c * 128 + 128],
                        ch(amT, c, tt * 512, tt * 512 + 512), start=(c == 0), stop=(c == NC - 1))
            p.wait("vector", "T", tm)
            ev = vstt(p, xtmp[:, sl * T + tt * 512: sl * T + tt * 512 + 512], bank(bi),
                      bdc[:, co:co + 1], xtmp[:, sl * T + tt * 512: sl * T + tt * 512 + 512],
                      ALU.add, ALU.add)
            p.setlast(("bank", bi), "V", ev)
        p.setlast(("wqb", sl), "T", tm)
        p.wait("sync", "V", ev)
        st = dma(p, "sync", x1o[co * 128:(co + 1) * 128, :], xtmp[:, sl * T:(sl + 1) * T],
                 inc="DT%d" % sl)
        p.setlast(("xtmp", sl), "DT%d" % sl, st)
        p.setlast(("x1st", co), "DT%d" % sl, st)
    return None



def emit_ffn1(ctx, xn, xn_ready, src_key="x1i", dst_key="y"):
    p = ctx.p
    bank = ctx.bank
    xr = ctx.sbuf("q1", [128, NC * T], BF16)
    xk = ctx.sbuf("k1", [128, NC * T], BF16)
    rr = ctx.sbuf("amT", [128, NC * T], BF16)
    kact = ctx.sbuf("am1b", [128, NC * T], BF16)
    xsh = ctx.sbuf("xsh", [128, T], BF16)
    kraw = ctx.sbuf("kraw", [128, 512], BF16)
    wqb = ctx.sbuf("wqb", [128, 2 * 1024], BF16)
    muc = ctx.sbuf("muc", [128, 16], F32)
    omuc = ctx.sbuf("omuc", [128, 16], F32)
    wr_r = ctx.dram["wrT"].rearrange("(cc p) o -> p cc o", p=128)
    wk_r = ctx.dram["wkT"].rearrange("(cc p) o -> p cc o", p=128)
    wv_r = ctx.dram["wvT"].rearrange("(cc p) o -> p cc o", p=128)

    dma(p, "sync", muc[:, 0:8], ctx.dram["murc"], inc=ctx.DSL)
    dma(p, "sync", muc[:, 8:16], ctx.dram["mukc"], inc=ctx.DSL)
    p.wait("vector", ctx.DSL, None)
    om = p.emit("vector", lambda e, o=omuc[:], i=muc[:]: e.tensor_scalar(o, i, -1.0, 1.0, ALU.mult, ALU.add),
                inc="V")

    mix_done = [0] * NC
    for c in range(NC):
        wait_all(p, "vector", [xn_ready[c]])
        p.guard("vector", ("xsh",))
        p.guard("gpsimd", ("xsh",))
        z = gmemset(p, xsh[:, 0:1], 0.0, inc="G")
        sh = vcopy(p, xsh[:, 1:T], ch(xn, c, 0, T - 1))
        p.wait("vector", "G", z)
        p.wait("vector", "V", om)
        a1 = vts(p, ch(xr, c), ch(xn, c), muc[:, c:c + 1], ALU.mult)
        a2 = vstt(p, ch(xr, c), xsh[:], omuc[:, c:c + 1], ch(xr, c), ALU.mult, ALU.add)
        b1 = vts(p, ch(xk, c), ch(xn, c), muc[:, 8 + c:8 + c + 1], ALU.mult)
        b2 = vstt(p, ch(xk, c), xsh[:], omuc[:, 8 + c:8 + c + 1], ch(xk, c), ALU.mult, ALU.add)
        p.setlast(("xsh",), "V", b2)
        mix_done[c] = b2

    for c in range(NC):
        p.wait("tensor", "V", mix_done[c])
    units = []
    for co in range(NC):
        units.append((wr_r, xr, rr, "sig", co))
        units.append((wk_r, xk, kact, "sq", co))
    for ui, (wsrc, src, dst, mode, co) in enumerate(units):
        sl = ui % 2
        p.guard("gpsimd", ("wqb", sl))
        g = dma(p, "gpsimd", wqb[:, sl * 1024:(sl + 1) * 1024].rearrange("p (cc o) -> p cc o", o=128),
                wsrc[:, :, co * 128:(co + 1) * 128], inc="DG%d" % sl)
        p.wait("tensor", "DG%d" % sl, g)
        tm = None
        for tt in range(NTT):
            bi = tt % 4
            p.guard("tensor", ("bank", bi))
            for c in range(NC):
                tm = mm(p, bank(bi), wqb[:, sl * 1024 + c * 128: sl * 1024 + c * 128 + 128],
                        ch(src, c, tt * 512, tt * 512 + 512), start=(c == 0), stop=(c == NC - 1))
            p.wait("scalar", "T", tm)
            if mode == "sig":
                ev = act(p, ch(dst, co, tt * 512, tt * 512 + 512), bank(bi), AF.Sigmoid)
                p.setlast(("bank", bi), "A", ev)
            else:
                p.guard("scalar", ("kraw",))
                ev = acopy(p, kraw[:], bank(bi))
                p.setlast(("bank", bi), "A", ev)
                p.wait("vector", "A", ev)
                kv = vstt(p, ch(dst, co, tt * 512, tt * 512 + 512), kraw[:], 0.0, kraw[:],
                          ALU.max, ALU.mult)
                p.setlast(("kraw",), "V", kv)
        p.setlast(("wqb", sl), "T", tm)
    kact_done = p.v("V")
    rr_done = p.v("A")

    xtmp = ctx.sbuf("xtmp", [128, 2 * T], F32)
    x1i = ctx.dram[src_key]
    yo = ctx.dram[dst_key]
    p.wait("tensor", "V", kact_done)
    for co in range(NC):
        sl = co % 2
        p.guard("gpsimd", ("wqb", sl))
        g = dma(p, "gpsimd", wqb[:, sl * 1024:(sl + 1) * 1024].rearrange("p (cc o) -> p cc o", o=128),
                wv_r[:, :, co * 128:(co + 1) * 128], inc="DG%d" % sl)
        p.wait("tensor", "DG%d" % sl, g)
        p.guard("sync", ("xtmp", sl))
        xl = dma(p, "sync", xtmp[:, sl * T:(sl + 1) * T], x1i[co * 128:(co + 1) * 128, :],
                 inc="DS%d" % sl)
        p.wait("vector", "DS%d" % sl, xl)
        p.wait("vector", "A", rr_done)
        tm = None
        for tt in range(NTT):
            bi = tt % 4
            p.guard("tensor", ("bank", bi))
            for c in range(NC):
                tm = mm(p, bank(bi), wqb[:, sl * 1024 + c * 128: sl * 1024 + c * 128 + 128],
                        ch(kact, c, tt * 512, tt * 512 + 512), start=(c == 0), stop=(c == NC - 1))
            p.wait("vector", "T", tm)
            p.guard("vector", ("tmpf",))
            t1 = vtt(p, ctx.sb["tmpf"][:], bank(bi), ch(rr, co, tt * 512, tt * 512 + 512), ALU.mult)
            p.setlast(("bank", bi), "V", t1)
            ev = vtt(p, xtmp[:, sl * T + tt * 512: sl * T + tt * 512 + 512],
                     xtmp[:, sl * T + tt * 512: sl * T + tt * 512 + 512], ctx.sb["tmpf"][:], ALU.add)
            p.setlast(("tmpf",), "V", ev)
        p.setlast(("wqb", sl), "T", tm)
        p.wait("sync", "V", ev)
        st = dma(p, "sync", yo[co * 128:(co + 1) * 128, :], xtmp[:, sl * T:(sl + 1) * T],
                 inc="DT%d" % sl)
        p.setlast(("xtmp", sl), "DT%d" % sl, st)
    return None



def emit_load_f32(ctx, tile, name):
    p = ctx.p
    src = ctx.dram[name]
    for c in range(NC):
        p.guard("sync", ("x1", c))
        dma(p, "sync", ch(tile, c), src[c * 128:(c + 1) * 128, :], inc=ctx.DSL)
    for c in range(NC):
        p.setlast(("x1", c), ctx.DSL, None)
    return [(ctx.DSL, None)] * NC


def emit_load_bf16(ctx, tile, name):
    p = ctx.p
    src = ctx.dram[name]
    for c in range(NC):
        dma(p, "sync", ch(tile, c), src[c * 128:(c + 1) * 128, :], inc=ctx.DSL)
    return [[(ctx.DSL, None)] for _ in range(NC)]


def emit_store(ctx, tile, ready, name):
    p = ctx.p
    dst = ctx.dram[name]
    last = None
    for c in range(NC):
        wait_all(p, "sync", [ready[c]] if isinstance(ready[c], tuple) else ready[c])
        last = dma(p, "sync", dst[c * 128:(c + 1) * 128, :], ch(tile, c), inc="DST")
    return last


def emit_resadd(ctx, x1, x1_ready, name):
    p = ctx.p
    tmpd = ctx.sbuf("tmpd", [128, 2 * T], F32)
    src = ctx.dram[name]
    out = []
    for c in range(NC):
        sl = c % 2
        p.guard("sync", ("tmpd", sl))
        m = dma(p, "sync", tmpd[:, sl * T:(sl + 1) * T], src[c * 128:(c + 1) * 128, :], inc="DS%d" % sl)
        p.wait("vector", "DS%d" % sl, m)
        wait_all(p, "vector", [x1_ready[c]] if isinstance(x1_ready[c], tuple) else x1_ready[c])
        ev = vtt(p, ch(x1, c), ch(x1, c), tmpd[:, sl * T:(sl + 1) * T], ALU.add)
        p.setlast(("tmpd", sl), "V", ev)
        p.setlast(("x1", c), "V", ev)
        out.append([("V", ev)])
    return out



def _finish(ctx):
    p = ctx.p
    for s in [n for n in list(p.val.keys()) if n.startswith("D")]:
        if p.v(s):
            p.wait("sync", s, p.v(s))
    ctx.p.finalize()


def phase_barrier(ctx, skip=()):
    p = ctx.p
    snap = {n: p.val[n] for n in list(p.val.keys()) if p.val[n] > 0}
    own = {"tensor": "T", "vector": "V", "scalar": "A", "gpsimd": "G"}
    for eng in ENGINES:
        if eng in skip:
            continue
        for n, v in snap.items():
            if own.get(eng) == n:
                continue
            p.wait(eng, n, v)


PHASE_WEIGHTS = {
    "A": [],
    "B": ["w1T", "w2T", "h1T", "h2T", "w3T", "s1c", "s2c", "sh1c", "sh2c", "s3c"],
    "C": ["wqkv0", "wout0", "hs0", "D3", "I3"],
    "D": [],
    "E": ["wqkv1", "bqkv1c", "bqkv1v", "wd1", "bd1c", "RT", "cosT", "sinT", "identb"],
    "F": ["wrT", "wkT", "wvT", "murc", "mukc"],
}

PHASE_IO = {
    "A": {"xT": ([C, T], F32, "ExternalInput"), "xn0": ([C, T], BF16, "ExternalOutput")},
    "B": {"xn0": ([C, T], BF16, "ExternalInput"), "ffn0": ([C, T], F32, "ExternalOutput")},
    "C": {"xn0": ([C, T], BF16, "ExternalInput"), "xT": ([C, T], F32, "ExternalInput"),
          "x1o": ([C, T], F32, "ExternalOutput")},
    "D": {"x1i": ([C, T], F32, "ExternalInput"), "ffn0": ([C, T], F32, "ExternalInput"),
          "x1o": ([C, T], F32, "ExternalOutput"), "xn1": ([C, T], BF16, "ExternalOutput")},
    "E": {"x1i": ([C, T], F32, "ExternalInput"), "xn1": ([C, T], BF16, "ExternalInput"),
          "x1o": ([C, T], F32, "ExternalOutput")},
    "F": {"x1i": ([C, T], F32, "ExternalInput"), "xn1": ([C, T], BF16, "ExternalInput"),
          "y": ([C, T], F32, "ExternalOutput")},
}


def build_phase(phase):
    nc = bass.Bass("TRN2", target_bir_lowering=False)
    es = ExitStack()
    with es:
        ctx = Ctx(nc, es, PHASE_WEIGHTS[phase], PHASE_IO[phase])
        ctx.commons()
        p = ctx.p
        if phase == "A":
            x1 = ctx.sbuf("x1", [128, NC * T], F32)
            xn = ctx.sbuf("xn", [128, NC * T], BF16)
            ctx.sbuf("sq", [128, NC * T], BF16)
            ctx.sbuf("rowbuf", [1, 4 * T], F32)
            ctx.sbuf("tmpf", [128, 512], F32)
            for c in range(NC):
                dma(p, "sync", ch(x1, c),
                    ctx.dram["xT"][c * 128:(c + 1) * 128, :], inc=ctx.DSL)
            ready = [[(ctx.DSL, None)] for _ in range(NC)]
            xn_ready = emit_ln(ctx, x1, xn, ready)
            emit_store(ctx, xn, xn_ready, "xn0")
        elif phase == "B":
            xn = ctx.sbuf("xn", [128, NC * T], BF16)
            xn_ready = emit_load_bf16(ctx, xn, "xn0")
            xn_ready = [r[0] for r in xn_ready]
            emit_ffn0(ctx, xn, xn_ready)
        elif phase == "C":
            xn = ctx.sbuf("xn", [128, NC * T], BF16)
            ctx.sbuf("rowbuf", [1, T + 512], F32)
            xn_ready = emit_load_bf16(ctx, xn, "xn0")
            xn_ready = [r[0] for r in xn_ready]
            emit_attn0(ctx, xn, xn_ready)
        elif phase == "D":
            x1 = ctx.sbuf("x1", [128, NC * T], F32)
            xn = ctx.sbuf("xn", [128, NC * T], BF16)
            ctx.sbuf("sq", [128, NC * T], BF16)
            ctx.sbuf("rowbuf", [1, 4 * T], F32)
            ctx.sbuf("tmpf", [128, 512], F32)
            x1_ready = emit_load_f32(ctx, x1, "x1i")
            rdy = emit_resadd(ctx, x1, x1_ready, "ffn0")
            xn_ready = emit_ln(ctx, x1, xn, rdy)
            emit_store(ctx, x1, rdy, "x1o")
            emit_store(ctx, xn, xn_ready, "xn1")
        elif phase == "E":
            xn = ctx.sbuf("xn", [128, NC * T], BF16)
            ctx.sbuf("rowbuf", [1, T + 512], F32)
            xn_ready = emit_load_bf16(ctx, xn, "xn1")
            xn_ready = [r[0] for r in xn_ready]
            emit_attn1(ctx, xn, xn_ready)
        elif phase == "F":
            xn = ctx.sbuf("xn", [128, NC * T], BF16)
            ctx.sbuf("tmpf", [128, 512], F32)
            xn_ready = emit_load_bf16(ctx, xn, "xn1")
            xn_ready = [r[0] for r in xn_ready]
            emit_ffn1(ctx, xn, xn_ready)
        _finish(ctx)
    return nc



_PHASE_CACHE = {}


def _get_phase(phase):
    if phase not in _PHASE_CACHE:
        _PHASE_CACHE[phase] = build_phase(phase)
    return _PHASE_CACHE[phase]


def kernel_unfused_legacy(**inputs):
    h = host_prep(inputs)
    ncores = 8
    cores = list(range(ncores))
    state = [dict() for _ in range(ncores)]

    def run(phase, in_names, weight_names, out_names):
        nc = _get_phase(phase)
        in_maps = []
        for b in range(ncores):
            m = {}
            for n in in_names:
                m[n] = state[b][n]
            for n in weight_names:
                m[n] = h[n]
            in_maps.append(m)
        res = run_bass_kernel_spmd(nc, in_maps, core_ids=cores)
        for b in range(ncores):
            for n in out_names:
                state[b][n] = np.asarray(res.results[b][n])

    for b in range(ncores):
        state[b]["xT"] = h["x_percore"][b]
    run("A", ["xT"], [], ["xn0"])
    run("B", ["xn0"], PHASE_WEIGHTS["B"], ["ffn0"])
    run("C", ["xn0", "xT"], PHASE_WEIGHTS["C"], ["x1o"])
    for b in range(ncores):
        state[b]["x1i"] = state[b].pop("x1o")
    run("D", ["x1i", "ffn0"], [], ["x1o", "xn1"])
    for b in range(ncores):
        state[b]["x1i"] = state[b].pop("x1o")
    run("E", ["x1i", "xn1"], PHASE_WEIGHTS["E"], ["x1o"])
    for b in range(ncores):
        state[b]["x1i"] = state[b].pop("x1o")
    run("F", ["x1i", "xn1"], PHASE_WEIGHTS["F"], ["y"])

    out = np.stack([np.ascontiguousarray(state[b]["y"].T) for b in range(ncores)], axis=0)
    return out.astype(np.float32)



FUSED_WEIGHTS = (PHASE_WEIGHTS["B"] + PHASE_WEIGHTS["C"] + PHASE_WEIGHTS["E"]
                 + PHASE_WEIGHTS["F"])

FUSED_IO = {
    "xT": ([C, T], F32, "ExternalInput"),
    "y": ([C, T], F32, "ExternalOutput"),
    "ffn0": ([C, T], F32, "Internal"),
    "x1b": ([C, T], F32, "Internal"),
    "x1c": ([C, T], F32, "Internal"),
    "vT1d": ([16 * 128, 1024], BF16, "Internal"),
}


def build_fused():
    nc = bass.Bass("TRN2", target_bir_lowering=False)
    es = ExitStack()
    with es:
        ctx = Ctx(nc, es, FUSED_WEIGHTS, FUSED_IO)
        ctx.commons()
        p = ctx.p
        xn = ctx.sbuf("xn", [128, NC * T], BF16)
        ctx.sbuf("wqb", [128, 2 * 1024], BF16)
        ctx.sbuf("stmp", [128, 2 * 512], BF16)
        ctx.sbuf("idt", [128, 128], BF16)
        ctx.sbuf("rt", [128, 128], BF16)
        R1 = ctx.sbuf("R1", [128, 65536], mybir.dt.uint8)
        R2 = ctx.sbuf("R2", [128, 65536], mybir.dt.uint8)
        R3 = ctx.sbuf("R3", [128, 32768], mybir.dt.uint8)

        def place(backing, name, shape, dt, byte_off):
            h = nc.virtual_sbuf_tensor(name, shape, dt)
            nc._tensor(h.name, h.shape, h.dtype, type="SB")
            mloc = nc.lookup_mloc(h)
            mloc.allocated = True
            base = nc.lookup_mloc(backing).addr
            nbytes = shape[1] * mybir.dt.size(dt)
            assert byte_off + nbytes <= backing.shape[1], (name, byte_off, nbytes)
            mloc.addr = base + byte_off
            ctx.sb[name] = h
            return h

        ctx.DSL = "DLA"
        srcA = place(R1, "srcA", [128, 16384], F32, 0)
        place(R3, "sq", [128, 16384], BF16, 0)
        place(R2, "rowbuf", [1, 8192], F32, 0)
        place(R2, "tmpf", [128, 512], F32, 32768)
        for c in range(NC):
            dma(p, "sync", ch(srcA, c), ctx.dram["xT"][c * 128:(c + 1) * 128, :], inc=ctx.DSL)
        ready = [[(ctx.DSL, None)] for _ in range(NC)]
        xn_ready = emit_ln(ctx, srcA, xn, ready)
        phase_barrier(ctx)

        ctx.DSL = "DLB"
        place(R1, "ha", [128, 32 * TF], BF16, 0)
        place(R2, "hb", [128, 32 * TF], BF16, 0)
        place(R3, "wbh", [128, 2 * 4096], BF16, 0)
        place(R3, "wb1", [128, 2 * 1024], BF16, 16384)
        place(R3, "wb2", [128, 2 * 1024], BF16, 20480)
        place(R3, "outst", [128, 2 * TF], F32, 24576)
        emit_ffn0(ctx, xn, xn_ready)
        phase_barrier(ctx)

        ctx.DSL = "DLC"
        place(R1, "q0", [128, NC * T], BF16, 0)
        place(R1, "k0", [128, NC * T], BF16, 32768)
        place(R2, "vT0", [128, 16 * 1024], BF16, 0)
        place(R2, "PT", [128, 16 * 512], BF16, 32768)
        place(R2, "xtmp", [128, 2 * T], F32, 49152)
        place(R2, "wvsec", [128, NC * 512], BF16, 49152)
        place(R3, "sqc", [128, 2 * T], BF16, 0)
        place(R3, "rowbuf", [1, T + 512], F32, 22528)
        place(R3, "rlb", [128, 512], F32, 20480)
        place(R1, "tmpd", [128, 2 * T], F32, 0)
        emit_attn0(ctx, xn, xn_ready, src_key="xT", dst_key="x1b")
        phase_barrier(ctx, skip=("sync",))

        ctx.DSL = "DLD"
        place(R2, "rowbuf", [1, 8192], F32, 0)
        place(R2, "tmpf", [128, 512], F32, 32768)
        place(R2, "xtmp", [128, 2 * T], F32, 49152)
        place(R3, "sq", [128, 16384], BF16, 0)
        xn_ready = emit_ln_stream(ctx, xn, ctx.sb["sq"], "x1b")
        phase_barrier(ctx)

        ctx.DSL = "DLF"
        place(R1, "q1", [128, NC * T], BF16, 0)
        place(R1, "k1", [128, NC * T], BF16, 32768)
        place(R2, "amT", [128, NC * T], BF16, 0)
        place(R2, "am1b", [128, NC * T], BF16, 32768)
        place(R3, "xtmp", [128, 2 * T], F32, 0)
        place(R3, "xsh", [128, T], BF16, 16384)
        place(R3, "kraw", [128, 512], BF16, 20480)
        place(R3, "muc", [128, 16], F32, 21504)
        place(R3, "omuc", [128, 16], F32, 21568)
        place(R3, "tmpf", [128, 512], F32, 22528)
        emit_ffn1(ctx, xn, xn_ready, src_key="x1b", dst_key="x1c")
        phase_barrier(ctx)

        ctx.DSL = "DLE"
        place(R1, "q1", [128, NC * T], BF16, 0)
        place(R1, "k1", [128, NC * T], BF16, 32768)
        place(R2, "vT1", [128, 16 * 1024], BF16, 0)
        place(R2, "PT", [128, 16 * 512], BF16, 32768)
        place(R2, "ET2", [128, 16 * 512], BF16, 49152)
        place(R2, "xtmp", [128, 2 * T], F32, 49152)
        place(R3, "CHst", [128, 40 * 128], BF16, 0)
        place(R3, "Ab", [128, 16 * 80], BF16, 10240)
        place(R3, "Ab2", [128, 16 * 80], BF16, 12800)
        place(R3, "yb", [128, 2 * 64], BF16, 15360)
        place(R3, "yb2", [128, 2 * 64], BF16, 15616)
        place(R3, "dgt", [128, 128], BF16, 15872)
        place(R3, "bqc", [128, 24], F32, 16128)
        place(R3, "bqs", [128, 8], F32, 16224)
        place(R3, "bdc", [128, 8], F32, 16256)
        place(R3, "tmb", [128, 2 * 512], BF16, 16288)
        place(R3, "bvb", [1, C], BF16, 16288)
        place(R3, "orb", [1, 128], BF16, 18336)
        place(R3, "wvsec", [128, NC * 512], BF16, 18592)
        place(R3, "rowbuf", [1, T], F32, 18592)
        place(R3, "vst", [128, 2 * 512], BF16, 26784)
        emit_attn1(ctx, xn, xn_ready, src_key="x1c", dst_key="y")
        _finish(ctx)
    return nc


_FUSED_CACHE = []


def kernel_fused(inputs):
    h = host_prep(inputs)
    if not _FUSED_CACHE:
        _FUSED_CACHE.append(build_fused())
    nc = _FUSED_CACHE[0]
    in_maps = []
    for b in range(8):
        m = {"xT": h["x_percore"][b]}
        for n in FUSED_WEIGHTS:
            m[n] = h[n]
        in_maps.append(m)
    res = run_bass_kernel_spmd(nc, in_maps, core_ids=list(range(8)))
    out = np.stack([np.ascontiguousarray(np.asarray(res.results[b]["y"]).T) for b in range(8)], axis=0)
    return out.astype(np.float32)


def kernel(**inputs):
    return kernel_fused(inputs)


# version 17
# speedup vs baseline: 1.0602x; 1.0602x over previous
import numpy as np
import ml_dtypes
from contextlib import ExitStack
import concourse.bass as bass
import concourse.mybir as mybir
from concourse.bass_utils import run_bass_kernel_spmd

F32 = mybir.dt.float32
BF16 = mybir.dt.bfloat16
AF = mybir.ActivationFunctionType
ALU = mybir.AluOpType
AX = mybir.AxisListType

T = 2048
C = 1024
NC = 8
HID = 4096
NH0, HD0 = 8, 128
NH1, HD1 = 16, 64
RD = 16
SCALE0 = 1.0 / float(np.sqrt(HD0))
SCALE1 = 1.0 / float(np.sqrt(HD1))
TF = 1024
NTT = 4
LN_EPS = 1e-5

ENGINES = ("sync", "gpsimd", "vector", "scalar", "tensor")


class Pl:

    def __init__(self, nc, es):
        self.nc = nc
        self.es = es
        self.q = {e: [] for e in ENGINES}
        self.sems = {}
        self.val = {}
        self.last = {}

    def sem(self, n):
        if n not in self.sems:
            self.sems[n] = self.es.enter_context(self.nc.semaphore(n))
            self.val[n] = 0
        return self.sems[n]

    def v(self, n):
        self.sem(n)
        return self.val[n]

    def wait(self, eng, n, v):
        s = self.sem(n)
        if v is None:
            self.q[eng].append(lambda e, s=s, n=n: e.wait_ge(s, self.val[n]))
        else:
            if eng in ("vector", "scalar") and self.AUTOSELF[eng] == n \
                    and v <= self.val.get(n, 0):
                return
            q = self.q[eng]
            prevtag = getattr(q[-1], "wtag", None) if q else None
            if prevtag is not None and prevtag[0] == n:
                if v <= prevtag[1]:
                    return
                q.pop()
            f = lambda e, s=s, v=v: e.wait_ge(s, v)
            f.wtag = (n, v)
            q.append(f)

    AUTOSELF = {"vector": "V", "scalar": "A", "gpsimd": "G"}

    def emit(self, eng, fn, inc=None, n=1):
        es_ = self.AUTOSELF.get(eng)
        if es_ is not None and self.val.get(es_, 0) > 0 and inc != "G0NOP":
            sw = self.sem(es_)
            vw = self.val[es_]
            self.q[eng].append(lambda e, s=sw, v=vw: e.wait_ge(s, v))
        if inc is None:
            self.q[eng].append(fn)
            return None
        s = self.sem(inc)
        self.val[inc] += n
        tgt = self.val[inc]
        self.q[eng].append(lambda e, fn=fn, s=s, n=n: fn(e).then_inc(s, n))
        return tgt

    ESEM = {"vector": "V", "scalar": "A", "gpsimd": "G", "tensor": "T"}

    def selfwait(self, eng):
        self.wait(eng, self.ESEM[eng], self.v(self.ESEM[eng]))

    def guard(self, eng, res):
        t = self.last.get(res)
        if t:
            self.wait(eng, t[0], t[1])

    def setlast(self, res, sem, val):
        self.last[res] = (sem, val)

    def finalize(self):
        with self.nc.Block() as block:
            @block.sync
            def _(e):
                for f in self.q["sync"]:
                    f(e)

            @block.gpsimd
            def _(e):
                for f in self.q["gpsimd"]:
                    f(e)

            @block.vector
            def _(e):
                for f in self.q["vector"]:
                    f(e)

            @block.scalar
            def _(e):
                for f in self.q["scalar"]:
                    f(e)

            @block.tensor
            def _(e):
                for f in self.q["tensor"]:
                    f(e)



def _bf(x):
    return np.ascontiguousarray(np.asarray(x), dtype=ml_dtypes.bfloat16)


def host_prep(inputs):
    f = np.float32
    h = {}
    x = np.asarray(inputs["L0_attn_x"], dtype=f)
    h["x_percore"] = [np.ascontiguousarray(x[b].T) for b in range(8)]

    h["wqkv0"] = np.ascontiguousarray(inputs["L0_attn_w_qkv"], dtype=f)
    h["wout0"] = np.ascontiguousarray(inputs["L0_attn_w_out"], dtype=f)
    h["hs0"] = np.asarray(inputs["L0_attn_head_scales"], dtype=f).reshape(1, 8)

    h["w1T"] = np.ascontiguousarray(np.asarray(inputs["L0_ffn_w1_w"], dtype=f).T)
    h["w2T"] = np.ascontiguousarray(np.asarray(inputs["L0_ffn_w2_w"], dtype=f).T)
    h["h1T"] = np.ascontiguousarray(np.asarray(inputs["L0_ffn_h1_w"], dtype=f).T)
    h["h2T"] = np.ascontiguousarray(np.asarray(inputs["L0_ffn_h2_w"], dtype=f).T)
    h["w3T"] = np.ascontiguousarray(np.asarray(inputs["L0_ffn_w3_w"], dtype=f).T)
    h["s1c"] = np.ascontiguousarray(np.asarray(inputs["L0_ffn_w1_s"], dtype=f).reshape(32, 128).T)
    h["s2c"] = np.ascontiguousarray(np.asarray(inputs["L0_ffn_w2_s"], dtype=f).reshape(32, 128).T)
    h["sh1c"] = np.ascontiguousarray(np.asarray(inputs["L0_ffn_h1_s"], dtype=f).reshape(32, 128).T)
    h["sh2c"] = np.ascontiguousarray(np.asarray(inputs["L0_ffn_h2_s"], dtype=f).reshape(32, 128).T)
    h["s3c"] = np.ascontiguousarray(np.asarray(inputs["L0_ffn_w3_s"], dtype=f).reshape(8, 128).T)

    h["wqkv1"] = np.ascontiguousarray(np.asarray(inputs["L1_attn_Wqkv"], dtype=f).T)
    h["bqkv1c"] = np.ascontiguousarray(np.asarray(inputs["L1_attn_bqkv"], dtype=f).reshape(24, 128).T)
    h["bqkv1v"] = np.ascontiguousarray(np.asarray(inputs["L1_attn_bqkv"], dtype=f)[2048:].reshape(1, 1024))
    h["wd1"] = np.ascontiguousarray(np.asarray(inputs["L1_attn_Wd"], dtype=f).T)
    h["bd1c"] = np.ascontiguousarray(np.asarray(inputs["L1_attn_bd"], dtype=f).reshape(8, 128).T)
    h["wrT"] = np.ascontiguousarray(np.asarray(inputs["L1_ffn_W_r"], dtype=f).T)
    h["wkT"] = np.ascontiguousarray(np.asarray(inputs["L1_ffn_W_k"], dtype=f).T)
    h["wvT"] = np.ascontiguousarray(np.asarray(inputs["L1_ffn_W_v"], dtype=f).T)
    h["murc"] = np.ascontiguousarray(np.asarray(inputs["L1_ffn_mu_r"], dtype=f).reshape(8, 128).T)
    h["mukc"] = np.ascontiguousarray(np.asarray(inputs["L1_ffn_mu_k"], dtype=f).reshape(8, 128).T)

    dirs = np.array([[(i >> 2 & 1) * 2 - 1, (i >> 1 & 1) * 2 - 1, (i & 1) * 2 - 1]
                     for i in range(8)], dtype=f) / np.sqrt(3.0, dtype=f)
    D3 = np.zeros((3, 24), dtype=f)
    for hh in range(8):
        D3[:, 3 * hh:3 * hh + 3] = np.outer(dirs[hh], dirs[hh])
    h["D3"] = D3
    h["I3"] = np.eye(3, dtype=f)
    h["identb"] = _bf(np.eye(128))
    R = np.zeros((128, 128), dtype=f)
    for base in (0, 64):
        for j in range(8):
            R[base + j, base + 8 + j] = -1.0
            R[base + 8 + j, base + j] = 1.0
    h["RT"] = _bf(R.T)
    invf = (10000.0 ** (-np.arange(0, RD, 2, dtype=np.float64) / RD)).astype(np.float64)
    t = np.arange(T, dtype=np.float64)
    cosrow = np.ones((128, T), dtype=f)
    sinrow = np.zeros((128, T), dtype=f)
    for base in (0, 64):
        for r in range(16):
            th = t * invf[r % 8]
            cosrow[base + r] = np.cos(th).astype(f)
            sinrow[base + r] = np.sin(th).astype(f)
    h["cosT"] = _bf(cosrow)
    h["sinT"] = _bf(sinrow)
    return h


WEIGHT_DECLS = {
    "wqkv0": ([C, 3 * C], F32), "wout0": ([C, C], F32), "hs0": ([1, 8], F32),
    "w1T": ([C, HID], F32), "w2T": ([C, HID], F32), "h1T": ([HID, HID], F32),
    "h2T": ([HID, HID], F32), "w3T": ([HID, C], F32),
    "s1c": ([128, 32], F32), "s2c": ([128, 32], F32), "sh1c": ([128, 32], F32),
    "sh2c": ([128, 32], F32), "s3c": ([128, 8], F32),
    "wqkv1": ([C, 3 * C], F32), "bqkv1c": ([128, 24], F32), "bqkv1v": ([1, C], F32),
    "wd1": ([C, C], F32), "bd1c": ([128, 8], F32),
    "wrT": ([C, C], F32), "wkT": ([C, C], F32), "wvT": ([C, C], F32),
    "murc": ([128, 8], F32), "mukc": ([128, 8], F32),
    "D3": ([3, 24], F32), "I3": ([3, 3], F32), "identb": ([128, 128], BF16),
    "RT": ([128, 128], BF16), "cosT": ([128, T], BF16), "sinT": ([128, T], BF16),
}


class Ctx:
    def __init__(self, nc, es, weight_names, io_decls):
        self.nc = nc
        self.es = es
        self.p = Pl(nc, es)
        self.dram = {}
        for n in weight_names:
            shape, dt = WEIGHT_DECLS[n]
            self.dram[n] = nc.dram_tensor(n, shape, dt, kind="ExternalInput").ap()
        for n, (shape, dt, kind) in io_decls.items():
            self.dram[n] = nc.dram_tensor(n, shape, dt, kind=kind).ap()
        self.sb = {}
        self.ps = {}
        self.DSL = "DSL"
        self.PW = [self.psum(f"PW{i}", [128, 1024], F32) for i in range(3)]
        self.PS6 = self.psum("PS6", [128, 512], F32)
        self.PTB = self.psum("PTB", [128, 512], BF16)

    def bank(self, i):
        if i < 6:
            return self.PW[i // 2][:, 512 * (i % 2): 512 * (i % 2) + 512]
        assert i == 6
        return self.PS6[:]

    def sbuf(self, name, shape, dt):
        if name not in self.sb:
            self.sb[name] = self.es.enter_context(self.nc.sbuf_tensor(name, shape, dt))
        return self.sb[name]

    def psum(self, name, shape, dt=F32):
        if name not in self.ps:
            self.ps[name] = self.es.enter_context(self.nc.psum_tensor(name, shape, dt))
        return self.ps[name]

    def commons(self):
        self.sbuf("onesb", [128, 1], BF16)
        self.sbuf("onesf", [1, 128], F32)
        self.sbuf("smalls", [1, 64], F32)
        self.sbuf("colb", [128, 48], F32)
        self.sbuf("epsc", [1, 1], F32)
        gmemset(self.p, self.sb["epsc"][:], LN_EPS)
        gmemset(self.p, self.sb["onesb"][:], 1.0)
        g0 = self.p.emit("gpsimd", lambda e, a=self.sb["onesf"][:]: e.memset(a, 1.0), inc="G")
        for eng in ("sync", "vector", "scalar", "tensor"):
            self.p.wait(eng, "G", g0)
        return g0



def mm(p, out, lhsT, rhs, start=True, stop=True, inc="T", n=1):
    return p.emit("tensor",
                  lambda e, o=out, l=lhsT, r=rhs, st=start, sp=stop: e.matmul(o, l, r, start=st, stop=sp),
                  inc=inc, n=n)


def tpose(p, out, in_, ident, inc="T"):
    return p.emit("tensor", lambda e, o=out, i=in_, d=ident: e.transpose(o, i, d), inc=inc)


def act(p, out, in_, func, bias=0.0, scale=1.0, accum=None, inc="A"):
    return p.emit("scalar",
                  lambda e, o=out, i=in_, f=func, b=bias, s=scale, a=accum:
                  e.activation(o, i, f, bias=b, scale=s, accum_out=a),
                  inc=inc)


def acopy(p, out, in_, inc="A"):
    return p.emit("scalar", lambda e, o=out, i=in_: e.copy(o, i), inc=inc)


def acopy_scale(p, out, in_, scale):
    return act(p, out, in_, AF.Copy, scale=scale)


def vcopy(p, out, in_, inc="V"):
    return p.emit("vector", lambda e, o=out, i=in_: e.tensor_copy(o, i), inc=inc)


def vtt(p, out, a, b, op, inc="V"):
    return p.emit("vector", lambda e, o=out, x=a, y=b, q=op: e.tensor_tensor(o, x, y, q), inc=inc)


def vts(p, out, in_, s1, op, inc="V"):
    return p.emit("vector", lambda e, o=out, i=in_, s=s1, q=op: e.tensor_scalar(o, i, s, None, q), inc=inc)


def vstt(p, out, in0, s, in1, op0, op1, inc="V"):
    return p.emit("vector",
                  lambda e, o=out, i0=in0, sc=s, i1=in1, a=op0, b=op1: e.scalar_tensor_tensor(o, i0, sc, i1, a, b),
                  inc=inc)


def vred(p, out, in_, op, absval=None, inc="V"):
    return p.emit("vector",
                  lambda e, o=out, i=in_, q=op, av=absval: e.tensor_reduce(o, i, AX.X, q, apply_absolute_value=av),
                  inc=inc)


def vrecip(p, out, in_, inc="V"):
    return p.emit("vector", lambda e, o=out, i=in_: e.reciprocal(o, i), inc=inc)


def dma(p, eng, out, in_, inc=None, n=16):
    return p.emit(eng, lambda e, o=out, i=in_: e.dma_start(out=o, in_=i), inc=inc, n=n)


def gmemset(p, ap, val, inc="G"):
    return p.emit("gpsimd", lambda e, a=ap, v=val: e.memset(a, v), inc=inc)


def gaffine(p, out, in_, cmp, fill, base, pattern, cm, inc="G"):
    return p.emit("gpsimd",
                  lambda e, o=out, i=in_, c=cmp, f=fill, b=base, pa=pattern, m=cm:
                  e.affine_select(out=o, in_=i, compare_op=c, fill=f, base=b, pattern=pa, channel_multiplier=m),
                  inc=inc)


def wait_all(p, eng, vals):
    for (s, v) in vals:
        p.wait(eng, s, v)


def ch(tile, c, a=0, b=T, rows=slice(0, 128)):
    return tile[rows, c * T + a: c * T + b]



def emit_ln(ctx, src, xn, ready):
    p = ctx.p
    bank = ctx.bank
    onesb = ctx.sb["onesb"]
    onesf = ctx.sb["onesf"]
    sq = ctx.sb["sq"]
    rows = ctx.sb["rowbuf"]
    tmpf = ctx.sb["tmpf"]

    cast_m = [None] * NC
    sq_m = [None] * NC
    for c in range(NC):
        wait_all(p, "vector", ready[c])
        wait_all(p, "scalar", ready[c])
        p.guard("vector", ("xn", c))
        cast_m[c] = vcopy(p, ch(xn, c), ch(src, c))
        p.guard("scalar", ("sqf", c))
        sq_m[c] = act(p, ch(sq, c), ch(src, c), AF.Square)
    for tt in range(NTT):
        for (bi, tile, res) in ((0, xn, 0), (1, sq, 2 * T)):
            p.guard("tensor", ("bank", bi))
            tm = None
            for c in range(NC):
                if tt == 0:
                    p.wait("tensor", "V" if bi == 0 else "A",
                           cast_m[c] if bi == 0 else sq_m[c])
                tm = mm(p, bank(bi)[0:1, :], onesb[:, 0:1], ch(tile, c, tt * 512, tt * 512 + 512),
                        start=(c == 0), stop=(c == NC - 1))
            p.wait("scalar", "T", tm)
            scale = (-1.0 / C) if bi == 0 else (1.0 / C)
            ev = acopy_scale(p, rows[0:1, res + tt * 512: res + tt * 512 + 512], bank(bi)[0:1, :], scale)
            p.setlast(("bank", bi), "A", ev)
    for c in range(NC):
        p.setlast(("sqf", c), "T", p.v("T"))
    p.selfwait("scalar")
    m2 = act(p, rows[0:1, 3 * T:4 * T], rows[0:1, 0:T], AF.Square)
    p.wait("vector", "A", m2)
    vsub = vtt(p, rows[0:1, 3 * T:4 * T], rows[0:1, 2 * T:3 * T], rows[0:1, 3 * T:4 * T], ALU.subtract)
    p.wait("scalar", "V", vsub)
    p.wait("scalar", "G", 0)
    act(p, rows[0:1, 3 * T:4 * T], rows[0:1, 3 * T:4 * T], AF.Ln, bias=ctx.sb["epsc"][0:1, 0:1])
    p.selfwait("scalar")
    rstd_m = act(p, rows[0:1, T:2 * T], rows[0:1, 3 * T:4 * T], AF.Exp, scale=-0.5)
    xn_ready = [None] * NC
    p.wait("tensor", "A", rstd_m)
    for tt in range(NTT):
        p.guard("tensor", ("bank", 4))
        p.guard("tensor", ("bank", 5))
        mm(p, bank(4), onesf[0:1, :], rows[0:1, tt * 512:tt * 512 + 512])
        b2 = mm(p, bank(5), onesf[0:1, :], rows[0:1, T + tt * 512:T + tt * 512 + 512])
        p.wait("vector", "T", b2)
        for c in range(NC):
            p.selfwait("vector")
            vtt(p, tmpf[:], ch(src, c, tt * 512, tt * 512 + 512), bank(4), ALU.add)
            p.selfwait("vector")
            xn_ready[c] = vtt(p, ch(xn, c, tt * 512, tt * 512 + 512), tmpf[:], bank(5), ALU.mult)
        p.setlast(("bank", 4), "V", xn_ready[NC - 1])
        p.setlast(("bank", 5), "V", xn_ready[NC - 1])
    for c in range(NC):
        p.setlast(("xn", c), "V", xn_ready[c])
    return [("V", xn_ready[c]) for c in range(NC)]


def emit_ln_stream(ctx, xn, sq, src_key):
    p = ctx.p
    bank = ctx.bank
    onesb = ctx.sb["onesb"]
    onesf = ctx.sb["onesf"]
    rows = ctx.sb["rowbuf"]
    tmpf = ctx.sb["tmpf"]
    xtmp = ctx.sb["xtmp"]
    srcd = ctx.dram[src_key]
    cast_ready = [None] * NC
    sq_ready = [None] * NC
    for c in range(NC):
        sl = c % 2
        p.guard("sync", ("xtD", sl))
        p.guard("sync", ("xtDv", sl))
        p.guard("sync", ("xtmp", sl))
        p.guard("sync", ("x1st", c))
        xl = dma(p, "sync", xtmp[:, sl * T:(sl + 1) * T], srcd[c * 128:(c + 1) * 128, :],
                 inc="DS%d" % sl)
        p.wait("vector", "DS%d" % sl, xl)
        p.guard("vector", ("xn", c))
        cast_ready[c] = vcopy(p, ch(xn, c), xtmp[:, sl * T:(sl + 1) * T])
        p.wait("scalar", "DS%d" % sl, xl)
        sq_ready[c] = act(p, ch(sq, c), xtmp[:, sl * T:(sl + 1) * T], AF.Square)
        p.setlast(("xtD", sl), "A", sq_ready[c])
        p.setlast(("xtDv", sl), "V", cast_ready[c])
    for c in range(NC):
        p.wait("tensor", "V", cast_ready[c])
        p.wait("tensor", "A", sq_ready[c])
    for tt in range(NTT):
        for (bi, tile, res) in ((0, xn, 0), (1, sq, 2 * T)):
            p.guard("tensor", ("bank", bi))
            tm = None
            for c in range(NC):
                tm = mm(p, bank(bi)[0:1, :], onesb[:, 0:1], ch(tile, c, tt * 512, tt * 512 + 512),
                        start=(c == 0), stop=(c == NC - 1))
            p.wait("scalar", "T", tm)
            scale = (-1.0 / C) if bi == 0 else (1.0 / C)
            ev = acopy_scale(p, rows[0:1, res + tt * 512: res + tt * 512 + 512], bank(bi)[0:1, :], scale)
            p.setlast(("bank", bi), "A", ev)
    p.selfwait("scalar")
    m2 = act(p, rows[0:1, 3 * T:4 * T], rows[0:1, 0:T], AF.Square)
    p.wait("vector", "A", m2)
    vsub = vtt(p, rows[0:1, 3 * T:4 * T], rows[0:1, 2 * T:3 * T], rows[0:1, 3 * T:4 * T], ALU.subtract)
    p.wait("scalar", "V", vsub)
    act(p, rows[0:1, 3 * T:4 * T], rows[0:1, 3 * T:4 * T], AF.Ln, bias=ctx.sb["epsc"][0:1, 0:1])
    p.selfwait("scalar")
    rstd_m = act(p, rows[0:1, T:2 * T], rows[0:1, 3 * T:4 * T], AF.Exp, scale=-0.5)
    xn_ready = [None] * NC
    p.wait("tensor", "A", rstd_m)
    for tt in range(NTT):
        p.guard("tensor", ("bank", 4))
        p.guard("tensor", ("bank", 5))
        mm(p, bank(4), onesf[0:1, :], rows[0:1, tt * 512:tt * 512 + 512])
        b2 = mm(p, bank(5), onesf[0:1, :], rows[0:1, T + tt * 512:T + tt * 512 + 512])
        p.wait("vector", "T", b2)
        for c in range(NC):
            p.selfwait("vector")
            vtt(p, tmpf[:], ch(xn, c, tt * 512, tt * 512 + 512), bank(4), ALU.add)
            p.selfwait("vector")
            xn_ready[c] = vtt(p, ch(xn, c, tt * 512, tt * 512 + 512), tmpf[:], bank(5), ALU.mult)
        p.setlast(("bank", 4), "V", xn_ready[NC - 1])
        p.setlast(("bank", 5), "V", xn_ready[NC - 1])
    for c in range(NC):
        p.setlast(("xn", c), "V", xn_ready[c])
    return [("V", xn_ready[c]) for c in range(NC)]



def emit_ffn0(ctx, xn, xn_ready):
    p = ctx.p
    bank = ctx.bank
    ha = ctx.sbuf("ha", [128, 32 * TF], BF16)
    hb = ctx.sbuf("hb", [128, 32 * TF], BF16)
    wb1 = ctx.sbuf("wb1", [128, 2 * 1024], BF16)
    wb2 = ctx.sbuf("wb2", [128, 2 * 1024], BF16)
    wbh = ctx.sbuf("wbh", [128, 2 * 4096], BF16)
    stmp = ctx.sbuf("stmp", [128, 2 * 512], BF16)
    outst = ctx.sbuf("outst", [128, 2 * TF], F32)
    s1c = ctx.sbuf("s1cs", [128, 32], F32)
    s2c = ctx.sbuf("s2cs", [128, 32], F32)
    sh1c = ctx.sbuf("sh1cs", [128, 32], F32)
    sh2c = ctx.sbuf("sh2cs", [128, 32], F32)
    s3c = ctx.sbuf("s3cs", [128, 8], F32)
    ffn0 = ctx.dram["ffn0"]
    for nm, tl in (("s1c", s1c), ("s2c", s2c), ("sh1c", sh1c), ("sh2c", sh2c), ("s3c", s3c)):
        dma(p, "sync", tl[:], ctx.dram[nm], inc=ctx.DSL)
    p.wait("scalar", ctx.DSL, None)

    w1T_r = ctx.dram["w1T"].rearrange("(cc p) m -> p cc m", p=128)
    w2T_r = ctx.dram["w2T"].rearrange("(cc p) m -> p cc m", p=128)
    h1T_r = ctx.dram["h1T"].rearrange("(kc p) m -> p kc m", p=128)
    h2T_r = ctx.dram["h2T"].rearrange("(kc p) m -> p kc m", p=128)
    w3T_r = ctx.dram["w3T"].rearrange("(kc p) m -> p kc m", p=128)

    wait_all(p, "tensor", xn_ready)
    for tf in range(T // TF):
        t0 = tf * TF
        for sl_ in range(2):
            p.setlast(("wb", sl_), "T", p.v("T"))
        for m in range(32):
            sl = m % 2
            p.guard("gpsimd", ("wb", sl))
            g1 = dma(p, "gpsimd", wb1[:, sl * 1024:(sl + 1) * 1024].rearrange("p (cc m) -> p cc m", m=128),
                     w1T_r[:, :, m * 128:(m + 1) * 128], inc="DGc%d" % sl)
            g = dma(p, "gpsimd", wb2[:, sl * 1024:(sl + 1) * 1024].rearrange("p (cc m) -> p cc m", m=128),
                    w2T_r[:, :, m * 128:(m + 1) * 128], inc="DG%d" % sl)
            p.wait("tensor", "DGc%d" % sl, g1)
            p.wait("tensor", "DG%d" % sl, g)
            tm = None
            for half in range(TF // 512):
                bs = 2 * (half % 2)
                p.guard("tensor", ("bank", bs))
                p.guard("tensor", ("bank", bs + 1))
                for c in range(NC):
                    mm(p, bank(bs), wb1[:, sl * 1024 + c * 128: sl * 1024 + c * 128 + 128],
                       ch(xn, c, t0 + half * 512, t0 + half * 512 + 512),
                       start=(c == 0), stop=(c == NC - 1))
                for c in range(NC):
                    tm = mm(p, bank(bs + 1), wb2[:, sl * 1024 + c * 128: sl * 1024 + c * 128 + 128],
                            ch(xn, c, t0 + half * 512, t0 + half * 512 + 512),
                            start=(c == 0), stop=(c == NC - 1))
                p.wait("scalar", "T", tm)
                p.guard("scalar", ("stmp", half))
                am = act(p, stmp[:, half * 512:(half + 1) * 512], bank(bs), AF.Silu, scale=s1c[:, m:m + 1])
                p.wait("vector", "A", am)
                vm = vstt(p, ha[:, m * TF + half * 512: m * TF + half * 512 + 512], bank(bs + 1),
                          s2c[:, m:m + 1], stmp[:, half * 512:(half + 1) * 512], ALU.mult, ALU.mult)
                p.setlast(("bank", bs), "V", vm)
                p.setlast(("bank", bs + 1), "V", vm)
                p.setlast(("stmp", half), "V", vm)
            p.setlast(("wb", sl), "T", tm)
        prev_sem, prev_val = "V", p.v("V")

        for (wr, sc, srct, dstt) in ((h1T_r, sh1c, ha, hb), (h2T_r, sh2c, hb, ha)):
            p.wait("tensor", prev_sem, prev_val)
            for sl_ in range(2):
                p.setlast(("wbh", sl_), "T", p.v("T"))
            for m2 in range(32):
                sl = m2 % 2
                p.guard("gpsimd", ("wbh", sl))
                g = dma(p, "gpsimd", wbh[:, sl * 4096:(sl + 1) * 4096].rearrange("p (kc m) -> p kc m", m=128),
                        wr[:, :, m2 * 128:(m2 + 1) * 128], inc="DG%d" % sl)
                p.wait("tensor", "DG%d" % sl, g)
                tm = None
                for half in range(TF // 512):
                    bi = (2 * m2 + half) % 4
                    p.guard("tensor", ("bank", bi))
                    for k in range(32):
                        tm = mm(p, bank(bi), wbh[:, sl * 4096 + k * 128: sl * 4096 + k * 128 + 128],
                                srct[:, k * TF + half * 512: k * TF + half * 512 + 512],
                                start=(k == 0), stop=(k == 31))
                    p.wait("scalar", "T", tm)
                    ev = act(p, dstt[:, m2 * TF + half * 512: m2 * TF + half * 512 + 512],
                             bank(bi), AF.Gelu, scale=sc[:, m2:m2 + 1])
                    p.setlast(("bank", bi), "A", ev)
                p.setlast(("wbh", sl), "T", tm)
            prev_sem, prev_val = "A", p.v("A")

        p.wait("tensor", "A", p.v("A"))
        for sl_ in range(2):
            p.setlast(("wbh", sl_), "T", p.v("T"))
        for mo in range(NC):
            sl = mo % 2
            p.guard("gpsimd", ("wbh", sl))
            g = dma(p, "gpsimd", wbh[:, sl * 4096:(sl + 1) * 4096].rearrange("p (kc m) -> p kc m", m=128),
                    w3T_r[:, :, mo * 128:(mo + 1) * 128], inc="DG%d" % sl)
            p.wait("tensor", "DG%d" % sl, g)
            tm = None
            for half in range(TF // 512):
                bi = (2 * mo + half) % 4
                p.guard("tensor", ("bank", bi))
                for k in range(32):
                    tm = mm(p, bank(bi), wbh[:, sl * 4096 + k * 128: sl * 4096 + k * 128 + 128],
                            ha[:, k * TF + half * 512: k * TF + half * 512 + 512],
                            start=(k == 0), stop=(k == 31))
                p.wait("scalar", "T", tm)
                p.guard("scalar", ("outst", (sl, half)))
                ev = act(p, outst[:, sl * TF + half * 512: sl * TF + half * 512 + 512],
                         bank(bi), AF.Copy, scale=s3c[:, mo:mo + 1])
                p.setlast(("bank", bi), "A", ev)
                p.wait("sync", "A", ev)
                sm = dma(p, "sync", ffn0[mo * 128:(mo + 1) * 128, t0 + half * 512: t0 + half * 512 + 512],
                         outst[:, sl * TF + half * 512: sl * TF + half * 512 + 512],
                         inc="DS%d_%d" % (sl, half))
                p.setlast(("outst", (sl, half)), "DS%d_%d" % (sl, half), sm)
            p.setlast(("wbh", sl), "T", tm)
    return p.v("DS0") + p.v("DS1")



def emit_attn0(ctx, xn, xn_ready, src_key="xT", dst_key="x1o"):
    p = ctx.p
    bank = ctx.bank
    q0 = ctx.sbuf("q0", [128, NC * T], BF16)
    k0 = ctx.sbuf("k0", [128, NC * T], BF16)
    vT0 = ctx.sbuf("vT0", [128, 16 * 1024], BF16)
    am0 = xn
    PT = ctx.sbuf("PT", [128, 16 * 512], BF16)
    wqb = ctx.sbuf("wqb", [128, 2 * 1024], BF16)
    wvsec = ctx.sbuf("wvsec", [128, NC * 512], BF16)
    rlb = ctx.sbuf("rlb", [128, 512], F32)
    hsrow = ctx.sbuf("hsrow", [1, 8], F32)
    d3t = ctx.sbuf("d3t", [3, 24], F32)
    i3t = ctx.sbuf("i3t", [3, 3], F32)
    b3 = ctx.sbuf("b3", [3, 24], BF16)
    c3 = ctx.sbuf("c3", [3, 8], F32)
    sq = PT
    rows = ctx.sb["rowbuf"]
    smalls = ctx.sb["smalls"]
    colb = ctx.sb["colb"]
    onesb = ctx.sb["onesb"]
    onesf = ctx.sb["onesf"]
    xtmp = ctx.sbuf("xtmp", [128, 2 * T], F32)
    xT = ctx.dram[src_key]
    x1o = ctx.dram[dst_key]

    wq_r = ctx.dram["wqkv0"].rearrange("(cc p) o -> p cc o", p=128)
    wo_r = ctx.dram["wout0"].rearrange("(cc p) o -> p cc o", p=128)

    dma(p, "sync", hsrow[:], ctx.dram["hs0"], inc=ctx.DSL)
    dma(p, "sync", d3t[:], ctx.dram["D3"], inc=ctx.DSL)
    dma(p, "sync", i3t[:], ctx.dram["I3"], inc=ctx.DSL)

    wait_all(p, "tensor", xn_ready)
    qk_done = [0] * 16
    for oc in range(16):
        sl = oc % 2
        p.guard("gpsimd", ("wqb", sl))
        g = dma(p, "gpsimd", wqb[:, sl * 1024:(sl + 1) * 1024].rearrange("p (cc o) -> p cc o", o=128),
                wq_r[:, :, oc * 128:(oc + 1) * 128], inc="DG%d" % sl)
        p.wait("tensor", "DG%d" % sl, g)
        tm = None
        for tt in range(NTT):
            bi = tt % 4
            p.guard("tensor", ("bank", bi))
            for c in range(NC):
                tm = mm(p, bank(bi), wqb[:, sl * 1024 + c * 128: sl * 1024 + c * 128 + 128],
                        ch(xn, c, tt * 512, tt * 512 + 512), start=(c == 0), stop=(c == NC - 1))
            p.wait("scalar", "T", tm)
            dst, cidx = (q0, oc) if oc < 8 else (k0, oc - 8)
            ev = act(p, ch(dst, cidx, tt * 512, tt * 512 + 512), bank(bi), AF.Copy,
                     scale=(SCALE0 if oc < 8 else 1.0))
            p.setlast(("bank", bi), "A", ev)
            qk_done[oc] = ev
        p.setlast(("wqb", sl), "T", tm)
    for dt in range(2):
        p.guard("gpsimd", ("wvsec",))
        wv = dma(p, "gpsimd", wvsec[:].rearrange("p (cc o) -> p cc o", o=512),
                 wq_r[:, :, 2048 + dt * 512: 2048 + dt * 512 + 512], inc="DGW")
        p.wait("tensor", "DGW", wv)
        tm = None
        for sb in range(16):
            bi = (sb * 2 + dt) % 4
            p.guard("tensor", ("bank", bi))
            for c in range(NC):
                tm = mm(p, bank(bi), ch(xn, c, sb * 128, sb * 128 + 128),
                        wvsec[:, c * 512: c * 512 + 512],
                        start=(c == 0), stop=(c == NC - 1))
            p.wait("scalar", "T", tm)
            ev = act(p, vT0[:, sb * 1024 + dt * 512: sb * 1024 + dt * 512 + 512], bank(bi), AF.Copy)
            p.setlast(("bank", bi), "A", ev)
        p.setlast(("wvsec",), "T", tm)

    p.wait("vector", ctx.DSL, None)
    p.wait("tensor", ctx.DSL, None)
    sqc = ctx.sbuf("sqc", [128, 2 * T], BF16)

    def setup_head(h):
        p.guard("tensor", ("bank", 4))
        cb = mm(p, bank(4)[0:3, 0:1], onesf[0:1, 0:3], hsrow[0:1, h:h + 1])
        p.wait("scalar", "T", cb)
        cm = acopy_scale(p, c3[:, h:h + 1], bank(4)[0:3, 0:1], float(np.sqrt(128.0)))
        p.setlast(("bank", 4), "A", cm)
        p.wait("vector", "A", cm)
        bm = vstt(p, b3[:, 3 * h:3 * h + 3], d3t[:, 3 * h:3 * h + 3], c3[:, h:h + 1],
                  i3t[:], ALU.mult, ALU.add)
        p.wait("tensor", "V", bm)
        p.wait("tensor", "A", qk_done[8 + h])
        ke = None
        for tt in range(NTT):
            p.guard("tensor", ("bank", 5))
            km = mm(p, bank(5)[0:3, :], b3[:, 3 * h:3 * h + 3],
                    ch(k0, h, tt * 512, tt * 512 + 512, rows=slice(0, 3)))
            p.wait("scalar", "T", km)
            ke = acopy(p, ch(k0, h, tt * 512, tt * 512 + 512, rows=slice(0, 3)), bank(5)[0:3, :])
            p.setlast(("bank", 5), "A", ke)
        p.guard("scalar", ("sqc", 0))
        act(p, sqc[:, 0:T], ch(q0, h), AF.Square)
        p.guard("scalar", ("sqc", 1))
        skm = act(p, sqc[:, T:2 * T], ch(k0, h), AF.Square)
        p.wait("tensor", "A", skm)
        r2 = mq = None
        for src_off, sm_i in ((0, 0), (T, 1)):
            rm = None
            for tt in range(NTT):
                bi = 4 + tt % 2
                p.guard("tensor", ("bank", bi))
                r2 = mm(p, bank(bi)[0:1, :], onesb[:, 0:1],
                        sqc[:, src_off + tt * 512:src_off + tt * 512 + 512])
                p.wait("scalar", "T", r2)
                p.guard("scalar", ("nrow", tt))
                rm = acopy(p, rows[0:1, tt * 512:tt * 512 + 512], bank(bi)[0:1, :])
                p.setlast(("bank", bi), "A", rm)
            p.wait("vector", "A", rm)
            mq = vred(p, smalls[0:1, sm_i:sm_i + 1], rows[0:1, 0:T], ALU.max)
            for tt in range(NTT):
                p.setlast(("nrow", tt), "V", mq)
        p.setlast(("sqc", 0), "T", r2)
        p.setlast(("sqc", 1), "T", r2)
        p.wait("scalar", "V", mq)
        sqr = act(p, smalls[0:1, 2:4], smalls[0:1, 0:2], AF.Sqrt)
        p.wait("vector", "A", sqr)
        vtt(p, smalls[0:1, 4:5], smalls[0:1, 2:3], smalls[0:1, 3:4], ALU.mult)
        p.selfwait("vector")
        ng = vts(p, smalls[0:1, 5:6], smalls[0:1, 4:5], -1.0, ALU.mult)
        p.wait("tensor", "V", ng)
        p.guard("tensor", ("bank", 4))
        nb = mm(p, bank(4)[:, 0:1], onesf[0:1, :], smalls[0:1, 5:6])
        p.wait("scalar", "T", nb)
        p.selfwait("scalar")
        negM = acopy(p, colb[:, h:h + 1], bank(4)[:, 0:1])
        p.setlast(("bank", 4), "A", negM)
        p.setlast(("negM0", h), "A", negM)

    def attn_head(h):
        for qt in range(NTT):
            nsb = 4 * qt + 4
            lsum = None
            pt_rdy = {}
            for sbp in range(0, nsb, 2):
                p.guard("tensor", ("bank", 4))
                p.guard("tensor", ("bank", 5))
                p.wait("tensor", "A", p.last[("negM0", h)][1])
                for d_ in range(2):
                    sb = sbp + d_
                    sm = mm(p, bank(4 + d_), k0[0:128, h * T + sb * 128: h * T + sb * 128 + 128],
                            ch(q0, h, qt * 512, qt * 512 + 512))
                p.wait("scalar", "T", sm)
                p.guard("scalar", ("PT", sbp))
                p.guard("scalar", ("PT", sbp + 1))
                ev = act(p, PT[:, sbp * 512:(sbp + 2) * 512], ctx.PW[2][:], AF.Exp,
                         bias=colb[:, h:h + 1])
                p.setlast(("bank", 4), "A", ev)
                p.setlast(("bank", 5), "A", ev)
                for d_ in range(2):
                    sb = sbp + d_
                    pt_rdy[sb] = ("A", ev)
                    if sb >= 4 * qt:
                        p.wait("gpsimd", "A", ev)
                        gm = gaffine(p, PT[:, sb * 512:(sb + 1) * 512],
                                     PT[:, sb * 512:(sb + 1) * 512], ALU.is_ge, 0.0,
                                     qt * 512 - sb * 128, [[1, 512]], -1)
                        pt_rdy[sb] = ("G", gm)
                for d_ in range(2):
                    sb = sbp + d_
                    p.wait("tensor", *pt_rdy[sb])
                    if sb == 0:
                        p.guard("tensor", ("bank", 6))
                    lsum = mm(p, bank(6)[0:1, :], onesb[:, 0:1], PT[:, sb * 512:(sb + 1) * 512],
                              start=(sb == 0), stop=(sb == nsb - 1))
            p.wait("vector", "T", lsum)
            rl = vrecip(p, rows[0:1, T:T + 512], bank(6)[0:1, :])
            p.setlast(("bank", 6), "V", rl)
            p.wait("tensor", "V", rl)
            p.guard("tensor", ("bank", 3))
            rb = mm(p, bank(3), onesf[0:1, :], rows[0:1, T:T + 512])
            p.wait("scalar", "T", rb)
            p.guard("scalar", ("rlb",))
            rlbm = acopy(p, rlb[:], bank(3))
            p.setlast(("bank", 3), "A", rlbm)
            pvb = qt % 2
            p.guard("tensor", ("bank", pvb))
            tm = None
            for sb in range(nsb):
                tm = mm(p, bank(pvb), vT0[:, sb * 1024 + h * 128: sb * 1024 + h * 128 + 128],
                        PT[:, sb * 512:(sb + 1) * 512], start=(sb == 0), stop=(sb == nsb - 1))
            for sb in range(nsb):
                p.setlast(("PT", sb), "T", tm)
            p.wait("vector", "T", tm)
            p.wait("vector", "A", rlbm)
            ev = vtt(p, ch(am0, h, qt * 512, qt * 512 + 512), bank(pvb), rlb[:], ALU.mult)
            p.setlast(("bank", pvb), "V", ev)
            p.setlast(("rlb",), "V", ev)

    for h in range(NH0):
        setup_head(h)
        attn_head(h)

    tmpd = ctx.sb["tmpd"] if "tmpd" in ctx.sb else ctx.sbuf("tmpd", [128, 2 * T], F32)
    ffn0d = ctx.dram["ffn0"]
    p.wait("tensor", "V", p.v("V"))
    p.guard("sync", ("wvsec",))
    p.setlast(("tmpd", 0), "T", p.v("T"))
    p.setlast(("tmpd", 1), "T", p.v("T"))
    cast_ready = [None] * NC
    sq_ready = [None] * NC
    for co in range(NC):
        sl = co % 2
        p.guard("gpsimd", ("wqb", sl))
        g = dma(p, "gpsimd", wqb[:, sl * 1024:(sl + 1) * 1024].rearrange("p (cc o) -> p cc o", o=128),
                wo_r[:, :, co * 128:(co + 1) * 128], inc="DG%d" % sl)
        p.wait("tensor", "DG%d" % sl, g)
        p.guard("sync", ("xtmp", sl))
        xl = dma(p, "sync", xtmp[:, sl * T:(sl + 1) * T], xT[co * 128:(co + 1) * 128, :],
                 inc="DS%d" % sl)
        p.guard("sync", ("tmpd", sl))
        fl = dma(p, "sync", tmpd[:, sl * T:(sl + 1) * T], ffn0d[co * 128:(co + 1) * 128, :],
                 inc="DF%d" % sl)
        p.wait("vector", "DS%d" % sl, xl)
        p.wait("vector", "DF%d" % sl, fl)
        tm = None
        for tt in range(NTT):
            bi = 4 + (tt % 2)
            p.guard("tensor", ("bank", bi))
            for c in range(NC):
                tm = mm(p, bank(bi), wqb[:, sl * 1024 + c * 128: sl * 1024 + c * 128 + 128],
                        ch(am0, c, tt * 512, tt * 512 + 512), start=(c == 0), stop=(c == NC - 1))
            p.wait("vector", "T", tm)
            sltt = slice(sl * T + tt * 512, sl * T + tt * 512 + 512)
            vtt(p, xtmp[:, sltt], xtmp[:, sltt], bank(bi), ALU.add)
            ev = vtt(p, xtmp[:, sltt], xtmp[:, sltt], tmpd[:, sltt], ALU.add)
            p.setlast(("bank", bi), "V", ev)
            p.setlast(("tmpd", sl), "V", ev)
        p.setlast(("wqb", sl), "T", tm)
        p.wait("sync", "V", ev)
        st = dma(p, "sync", x1o[co * 128:(co + 1) * 128, :], xtmp[:, sl * T:(sl + 1) * T],
                 inc="DT%d" % sl)
        p.setlast(("xtmp", sl), "DT%d" % sl, st)
        p.setlast(("x1st", co), "DT%d" % sl, st)
    return None



def emit_attn1(ctx, xn, xn_ready, src_key="x1i", dst_key="x1o"):
    p = ctx.p
    bank = ctx.bank
    q1 = ctx.sbuf("q1", [128, NC * T], BF16)
    k1 = ctx.sbuf("k1", [128, NC * T], BF16)
    vT1 = ctx.sbuf("vT1", [128, 16 * 1024], BF16)
    am1 = xn[:, 0:16 * 1024]
    amT = q1
    ET = ctx.sbuf("PT", [128, 16 * 512], BF16)
    Ab = ctx.sbuf("Ab", [128, 16 * 80], BF16)
    yb = ctx.sbuf("yb", [128, 2 * 64], BF16)
    dgt = ctx.sbuf("dgt", [128, 128], BF16)
    tmb = ctx.sbuf("tmb", [128, 2 * 512], BF16)
    wqb = ctx.sbuf("wqb", [128, 2 * 1024], BF16)
    wvsec = ctx.sbuf("wvsec", [128, NC * 512], BF16)
    idt = ctx.sbuf("idt", [128, 128], BF16)
    rt = ctx.sbuf("rt", [128, 128], BF16)
    cost = ET[:, 0:T]
    sint = ET[:, T:2 * T]
    bqc = ctx.sbuf("bqc", [128, 24], F32)
    bqs = ctx.sbuf("bqs", [128, 8], F32)
    bdc = ctx.sbuf("bdc", [128, 8], F32)
    bvb = ctx.sbuf("bvb", [1, C], BF16)
    orb = ctx.sbuf("orb", [1, 128], BF16)
    sq = am1
    rows = ctx.sb["rowbuf"]
    smalls = ctx.sb["smalls"]
    colb = ctx.sb["colb"]
    onesb = ctx.sb["onesb"]
    onesf = ctx.sb["onesf"]
    PTB = ctx.PTB

    wq_r = ctx.dram["wqkv1"].rearrange("(cc p) o -> p cc o", p=128)
    wd_r = ctx.dram["wd1"].rearrange("(cc p) o -> p cc o", p=128)

    dma(p, "sync", bqc[:], ctx.dram["bqkv1c"], inc=ctx.DSL)
    dma(p, "sync", bdc[:], ctx.dram["bd1c"], inc=ctx.DSL)
    bvf = ctx.sb["rowbuf"][0:1, 0:C]
    dma(p, "sync", bvf, ctx.dram["bqkv1v"], inc=ctx.DSL)
    dma(p, "sync", idt[:], ctx.dram["identb"], inc=ctx.DSL)
    dma(p, "sync", rt[:], ctx.dram["RT"], inc=ctx.DSL)
    dma(p, "sync", cost, ctx.dram["cosT"], inc=ctx.DSL)
    dma(p, "sync", sint, ctx.dram["sinT"], inc=ctx.DSL)
    gmemset(p, orb[:], 1.0)
    for i in range(16):
        gmemset(p, Ab[:, i * 80 + 64:i * 80 + 65], 1.0)
    ab_ones = p.emit("gpsimd", lambda e, a=Ab[0:1, 0:1]: e.nop(), inc="G")
    p.wait("vector", ctx.DSL, None)
    bq = vts(p, bqs[:], bqc[:, 0:8], SCALE1, ALU.mult)
    bv = vcopy(p, bvb[:], bvf)
    p.setlast(("nrow", 0), "V", bv)
    p.setlast(("nrow", 1), "V", bv)
    p.setlast(("wvsec",), "V", bv)

    wait_all(p, "tensor", xn_ready)
    p.wait("tensor", ctx.DSL, None)
    qk_done = [0] * 16
    for oc in range(16):
        sl = oc % 2
        p.guard("gpsimd", ("wqb", sl))
        g = dma(p, "gpsimd", wqb[:, sl * 1024:(sl + 1) * 1024].rearrange("p (cc o) -> p cc o", o=128),
                wq_r[:, :, oc * 128:(oc + 1) * 128], inc="DG%d" % sl)
        p.wait("tensor", "DG%d" % sl, g)
        tm = None
        for tt in range(NTT):
            bi = tt % 4
            p.guard("tensor", ("bank", bi))
            for c in range(NC):
                tm = mm(p, bank(bi), wqb[:, sl * 1024 + c * 128: sl * 1024 + c * 128 + 128],
                        ch(xn, c, tt * 512, tt * 512 + 512), start=(c == 0), stop=(c == NC - 1))
            p.wait("scalar", "T", tm)
            if oc < 8:
                p.wait("scalar", "V", bq)
                ev = act(p, ch(q1, oc, tt * 512, tt * 512 + 512), bank(bi), AF.Identity,
                         scale=SCALE1, bias=bqs[:, oc:oc + 1])
            else:
                ev = act(p, ch(k1, oc - 8, tt * 512, tt * 512 + 512), bank(bi), AF.Identity,
                         bias=bqc[:, oc:oc + 1])
            p.setlast(("bank", bi), "A", ev)
            qk_done[oc] = ev
        p.setlast(("wqb", sl), "T", tm)
    p.wait("tensor", "V", bv)
    p.wait("tensor", "G", ab_ones)
    for dt in range(2):
        p.guard("gpsimd", ("wvsec",))
        wv = dma(p, "gpsimd", wvsec[:].rearrange("p (cc o) -> p cc o", o=512),
                 wq_r[:, :, 2048 + dt * 512: 2048 + dt * 512 + 512], inc="DGW")
        p.wait("tensor", "DGW", wv)
        tm = None
        for sb in range(16):
            bi = (sb * 2 + dt) % 4
            p.guard("tensor", ("bank", bi))
            for c in range(NC):
                mm(p, bank(bi), ch(xn, c, sb * 128, sb * 128 + 128),
                   wvsec[:, c * 512: c * 512 + 512],
                   start=(c == 0), stop=False)
            tm = mm(p, bank(bi), orb[0:1, :], bvb[0:1, dt * 512:dt * 512 + 512],
                    start=False, stop=True)
            p.wait("scalar", "T", tm)
            ev = act(p, vT1[:, sb * 1024 + dt * 512: sb * 1024 + dt * 512 + 512], bank(bi), AF.Copy)
            p.setlast(("bank", bi), "A", ev)
            p.setlast(("vte", sb, dt), "A", ev)
        p.setlast(("wvsec",), "T", tm)
    vT1d = ctx.dram["vT1d"]
    for sb in range(16):
        p.wait("sync", *p.last[("vte", sb, 0)])
        p.wait("sync", *p.last[("vte", sb, 1)])
        dma(p, "sync", vT1d[sb * 128:(sb + 1) * 128, :],
            vT1[:, sb * 1024:(sb + 1) * 1024], inc="DVT")

    rope_done = [0] * 16
    for gi, gt in enumerate((q1, k1)):
        for cc in range(NC):
            p.wait("tensor", "A", qk_done[gi * 8 + cc])
            for tt in range(NTT):
                bi = 4 + (tt % 2)
                p.guard("tensor", ("bank", bi))
                pm = mm(p, bank(bi), rt[:], ch(gt, cc, tt * 512, tt * 512 + 512))
                p.wait("vector", "T", pm)
                p.guard("vector", ("tmb", 0))
                t1 = vtt(p, tmb[:, 0:512], ch(gt, cc, tt * 512, tt * 512 + 512),
                         cost[:, tt * 512:tt * 512 + 512], ALU.mult)
                p.guard("vector", ("tmb", 1))
                t2 = vtt(p, tmb[:, 512:1024], bank(bi), sint[:, tt * 512:tt * 512 + 512], ALU.mult)
                p.setlast(("bank", bi), "V", t2)
                ev = vtt(p, ch(gt, cc, tt * 512, tt * 512 + 512), tmb[:, 0:512], tmb[:, 512:1024],
                         ALU.add)
                p.setlast(("tmb", 0), "V", ev)
                p.setlast(("tmb", 1), "V", ev)
                rope_done[gi * 8 + cc] = ev

    for cc in range(NC):
        p.wait("scalar", "V", rope_done[cc])
        p.wait("scalar", "V", rope_done[8 + cc])
        act(p, sq[:, 0:T], ch(q1, cc), AF.Square)
        skm = act(p, sq[:, T:2 * T], ch(k1, cc), AF.Square)
        p.wait("tensor", "A", skm)
        for h2 in range(2):
            h = 2 * cc + h2
            r0 = 64 * h2
            for src_off, sm_i in ((0, 0), (T, 1)):
                rm = None
                for tt in range(NTT):
                    bi = 4 + tt % 2
                    p.guard("tensor", ("bank", bi))
                    r2 = mm(p, bank(bi)[0:1, :], onesb[r0:r0 + 64, 0:1],
                            sq[r0:r0 + 64, src_off + tt * 512:src_off + tt * 512 + 512])
                    p.wait("scalar", "T", r2)
                    p.guard("scalar", ("nrow", tt))
                    rm = acopy(p, rows[0:1, tt * 512:tt * 512 + 512], bank(bi)[0:1, :])
                    p.setlast(("bank", bi), "A", rm)
                p.wait("vector", "A", rm)
                mq = vred(p, smalls[0:1, sm_i:sm_i + 1], rows[0:1, 0:T], ALU.max)
                for tt in range(NTT):
                    p.setlast(("nrow", tt), "V", mq)
            p.wait("scalar", "V", mq)
            sqr = act(p, smalls[0:1, 2:4], smalls[0:1, 0:2], AF.Sqrt)
            p.wait("vector", "A", sqr)
            vtt(p, smalls[0:1, 4:5], smalls[0:1, 2:3], smalls[0:1, 3:4], ALU.mult)
            ng = vts(p, smalls[0:1, 5:6], smalls[0:1, 4:5], -1.0, ALU.mult)
            p.wait("tensor", "V", ng)
            p.guard("tensor", ("bank", 4))
            nb = mm(p, bank(4)[:, 0:1], onesf[0:1, :], smalls[0:1, 5:6])
            p.wait("scalar", "T", nb)
            negM = acopy(p, colb[:, h:h + 1], bank(4)[:, 0:1])
            p.setlast(("bank", 4), "A", negM)
            p.setlast(("negM", h), "A", negM)

    ET2 = ctx.sbuf("ET2", [128, 16 * 512], BF16)
    Ab2 = ctx.sbuf("Ab2", [128, 16 * 80], BF16)
    yb2 = ctx.sbuf("yb2", [128, 2 * 64], BF16)
    for i in range(16):
        gmemset(p, Ab2[:, i * 80 + 64:i * 80 + 65], 1.0)
    ab2_ones = p.emit("gpsimd", lambda e, a=Ab2[0:1, 0:1]: e.nop(), inc="G")
    p.wait("tensor", "G", ab2_ones)
    rope_all = max(rope_done)
    p.wait("scalar", "V", rope_all)
    CH = ctx.sbuf("CHst", [128, 40 * 128], BF16)
    vst = ctx.sbuf("vst", [128, 2 * 512], BF16)
    HS = ({"Ab": Ab, "yb": yb}, {"Ab": Ab2, "yb": yb2})

    def chsl(chain, slot):
        o = (slot * 8 + chain) * 128
        return CH[:, o:o + 128]

    def chgrp(grp, slot):
        o = (slot * 8 + grp * 4) * 128
        return CH[:, o:o + 512]

    def etsl(half, hi, a, b):
        if half == 0:
            t = ET if hi == 0 else ET2
            return t[:, a:b]
        o = hi * 8192
        return vT1[:, o + a: o + b]

    evict_waited = [False]

    def prep_chunks(hp, pn, half, out):
        nsb = 4 * pn + 4
        lst = []

        def exp_unit(sbp, hi):
            def emit():
                h = 2 * hp + hi
                r0 = 64 * hi
                p.guard("tensor", ("bank", 4))
                p.guard("tensor", ("bank", 5))
                p.wait("tensor", "A", p.last[("negM", h)][1])
                smm = None
                for d_ in range(2):
                    sb = sbp + d_
                    smm = mm(p, bank(4 + d_),
                             k1[r0:r0 + 64, hp * T + sb * 128: hp * T + sb * 128 + 128],
                             q1[r0:r0 + 64, hp * T + pn * 512: hp * T + pn * 512 + 512])
                p.wait("scalar", "T", smm)
                if half == 1 and not evict_waited[0]:
                    evict_waited[0] = True
                    p.wait("scalar", "DVT", None)
                for d_ in range(2):
                    p.guard("scalar", ("PT", half, hi, sbp + d_))
                    p.guard("scalar", ("PTv", half, hi, sbp + d_))
                ev = act(p, etsl(half, hi, sbp * 512, (sbp + 2) * 512), ctx.PW[2][:], AF.Exp,
                         bias=colb[:, h:h + 1])
                p.setlast(("bank", 4), "A", ev)
                p.setlast(("bank", 5), "A", ev)
                for d_ in range(2):
                    sb = sbp + d_
                    p.setlast(("ET", half, hi, sb), "A", ev)
                    if sb >= 4 * pn:
                        p.wait("gpsimd", "A", ev)
                        gm = gaffine(p, etsl(half, hi, sb * 512, (sb + 1) * 512),
                                     etsl(half, hi, sb * 512, (sb + 1) * 512), ALU.is_ge, 0.0,
                                     pn * 512 - sb * 128, [[1, 512]], -1)
                        p.setlast(("ET", half, hi, sb), "G", gm)
            return emit

        for sbp in range(0, nsb, 2):
            for hi in range(2):
                lst.append(exp_unit(sbp, hi))

        def l_unit(hi, ii):
            def emit():
                i = 4 * pn + ii
                col = hi * 4 + ii
                lb = col % 2
                p.guard("tensor", ("bank", lb))
                lm = None
                for j in range(i + 1):
                    p.wait("tensor", *p.last[("ET", half, hi, j)])
                    lm = mm(p, bank(lb)[:, 0:1],
                            etsl(half, hi, j * 512 + ii * 128, j * 512 + ii * 128 + 128),
                            onesb[:, 0:1], start=(j == 0), stop=(j == i))
                p.wait("vector", "T", lm)
                lc_ = 16 + half * 16 + col
                rc = vrecip(p, colb[:, lc_:lc_ + 1], bank(lb)[:, 0:1])
                p.setlast(("bank", lb), "V", rc)
                out["lrec"][(hi, ii)] = colb[:, lc_:lc_ + 1]
                p.guard("vector", ("dgt",))
                vtt(p, dgt[:], etsl(half, hi, i * 512 + ii * 128, i * 512 + ii * 128 + 128),
                    idt[:], ALU.mult)
                dc_ = 24 + half * 16 + col
                dg = vred(p, colb[:, dc_:dc_ + 1], dgt[:], ALU.add)
                p.setlast(("dgt",), "V", dg)
                out["dgcol"][(hi, ii)] = (colb[:, dc_:dc_ + 1], dg)
            return emit

        for hi in range(2):
            for ii in range(4):
                lst.append(l_unit(hi, ii))
        return lst

    def chains_panel(hp, pn, half, lrec):
        vstate = {}
        for hi in range(2):
            for ii in range(4):
                chn = hi * 4 + ii
                i = 4 * pn + ii
                p.guard("vector", ("ch", chn, 0))
                v0 = vts(p, chsl(chn, 0),
                         etsl(half, hi, i * 512 + ii * 128, i * 512 + ii * 128 + 128),
                         lrec[(hi, ii)], ALU.mult)
                p.wait("gpsimd", "V", v0)
                v0g = gaffine(p, chsl(chn, 0), chsl(chn, 0), ALU.is_ge, 0.0, -1, [[1, 128]], -1)
                vstate[chn] = {"v": ("G", v0g)}
        for grp in range(2):
            p.guard("tensor", ("PTB",))
            tp = None
            for lc in range(4):
                chn = grp * 4 + lc
                p.wait("tensor", *vstate[chn]["v"])
                tp = tpose(p, PTB[:, lc * 128:(lc + 1) * 128], chsl(chn, 0), idt[:])
            p.wait("scalar", "T", tp)
            for lc in range(4):
                p.guard("scalar", ("ch", grp * 4 + lc, 2))
            ycp = acopy(p, chgrp(grp, 2), PTB[:])
            p.setlast(("PTB",), "A", ycp)
            for lc in range(4):
                chn = grp * 4 + lc
                vstate[chn]["y"] = ("A", ycp)
                p.wait("vector", *vstate[chn]["v"])
                p.guard("vector", ("ch", chn, 4))
                x0 = vtt(p, chsl(chn, 4), chsl(chn, 0), idt[:], ALU.add)
                vstate[chn]["x"] = ("V", x0)
        for ml in range(1, 7):
            vc = (ml - 1) % 2
            vn = 1 - vc
            for grp in range(2):
                bv_, by_ = grp, 2 + grp
                p.guard("tensor", ("bank", bv_))
                p.guard("tensor", ("bank", by_))
                tmv = None
                for lc in range(4):
                    chn = grp * 4 + lc
                    s = vstate[chn]
                    p.wait("tensor", *s["v"])
                    p.wait("tensor", *s["y"])
                    mm(p, bank(bv_)[:, lc * 128:(lc + 1) * 128],
                       chsl(chn, 2 + vc), chsl(chn, vc))
                    tmv = mm(p, bank(by_)[:, lc * 128:(lc + 1) * 128],
                             chsl(chn, vc), chsl(chn, 2 + vc))
                eng = "vector" if grp else "scalar"
                esem = "V" if grp else "A"
                p.wait(eng, "T", tmv)
                p.guard(eng, ("chg", grp, vn))
                p.guard(eng, ("chg", grp, 2 + vn))
                if eng == "vector":
                    e1 = vcopy(p, chgrp(grp, vn), bank(bv_))
                    e2 = vcopy(p, chgrp(grp, 2 + vn), bank(by_))
                else:
                    e1 = acopy(p, chgrp(grp, vn), bank(bv_))
                    e2 = acopy(p, chgrp(grp, 2 + vn), bank(by_))
                p.setlast(("bank", bv_), esem, e2)
                p.setlast(("bank", by_), esem, e2)
                p.setlast(("chg", grp, vc), "T", tmv)
                p.setlast(("chg", grp, 2 + vc), "T", tmv)
                for lc in range(4):
                    chn = grp * 4 + lc
                    vstate[chn]["v"] = (esem, e1)
                    vstate[chn]["y"] = (esem, e2)
            for grp in range(2):
                bx = 4 + grp
                p.guard("tensor", ("bank", bx))
                tmx = None
                for lc in range(4):
                    chn = grp * 4 + lc
                    s = vstate[chn]
                    p.wait("tensor", *s["y"])
                    p.wait("tensor", *s["x"])
                    tmx = mm(p, bank(bx)[:, lc * 128:(lc + 1) * 128],
                             chsl(chn, 2 + vn), chsl(chn, 4))
                p.setlast(("chg", grp, 2 + vn), "T", tmx)
                p.wait("vector", "T", tmx)
                p.guard("vector", ("chg", grp, 4))
                xe = vtt(p, chgrp(grp, 4), bank(bx), chgrp(grp, 4), ALU.add)
                p.setlast(("bank", bx), "V", xe)
                for lc in range(4):
                    vstate[grp * 4 + lc]["x"] = ("V", xe)
        return vstate

    def seq_panel(hp, pn, half, lrec, dgcol, vstate, vsl, chunks):
        nblk = 0
        L = len(chunks)
        p.wait("vector", "DV%d" % vsl, p.last[("vstl", vsl)][1])
        for ii in range(4):
            i = 4 * pn + ii
            for hi in range(2):
                h = 2 * hp + hi
                st = HS[hi]
                chn = hi * 4 + ii
                pcb = 2 + hi
                p.guard("tensor", ("bank", pcb))
                lm = None
                for j in range(i):
                    p.wait("tensor", *p.last[("ET", half, hi, j)])
                    p.guard("tensor", ("Ab", hi, j))
                    lm = mm(p, bank(pcb)[:, 0:64],
                            etsl(half, hi, j * 512 + ii * 128, j * 512 + ii * 128 + 128),
                            st["Ab"][:, j * 80: j * 80 + 64], start=(j == 0), stop=(j == i - 1))
                dgc, dgm = dgcol[(hi, ii)]
                p.wait("vector", "V", dgm)
                vo = vsl * 512 + ii * 128 + hi * 64
                if i > 0:
                    p.wait("vector", "T", lm)
                    p.guard("vector", ("yb", hi, 0))
                    y0 = vstt(p, st["yb"][:, 0:64], vst[:, vo:vo + 64],
                              dgc, bank(pcb)[:, 0:64], ALU.mult, ALU.add)
                else:
                    p.guard("vector", ("yb", hi, 0))
                    y0 = vts(p, st["yb"][:, 0:64], vst[:, vo:vo + 64], dgc, ALU.mult)
                p.setlast(("bank", pcb), "V", y0)
                p.guard("tensor", ("bank", 6))
                p.wait("tensor", "V", y0)
                p.wait("tensor", *vstate[chn]["x"])
                bm_ = mm(p, bank(6)[:, 0:64], chsl(chn, 4), st["yb"][:, 0:64])
                p.wait("vector", "T", bm_)
                ai = vts(p, st["Ab"][:, i * 80: i * 80 + 64], bank(6)[:, 0:64],
                         lrec[(hi, ii)], ALU.mult)
                p.setlast(("bank", 6), "V", ai)
                p.setlast(("Ab", hi, i), "V", ai)
                p.setlast(("yb", hi, 0), "V", ai)
                p.wait("scalar", "V", ai)
                acopy(p, am1[:, i * 1024 + h * 64: i * 1024 + h * 64 + 64],
                      st["Ab"][:, i * 80: i * 80 + 64])
                for sl_ in range(5):
                    p.setlast(("ch", chn, sl_), "V", ai)
                    p.setlast(("chg", chn // 4, sl_), "V", ai)
                nblk += 1
                for c_ in chunks[(nblk - 1) * L // 8: nblk * L // 8]:
                    c_()
        for sb in range(4 * pn + 4):
            for hi in range(2):
                p.setlast(("PT", half, hi, sb), "T", p.v("T"))
                p.setlast(("PTv", half, hi, sb), "V", p.v("V"))
        p.setlast(("vst", vsl), "V", p.v("V"))

    def load_vst(hp, pn, vsl):
        p.guard("sync", ("vst", vsl))
        lv = None
        for ii in range(4):
            i = 4 * pn + ii
            lv = dma(p, "sync", vst[:, vsl * 512 + ii * 128: vsl * 512 + (ii + 1) * 128],
                     vT1d[i * 128:(i + 1) * 128, hp * 128:(hp + 1) * 128],
                     inc="DV%d" % vsl)
        p.setlast(("vstl", vsl), "DV%d" % vsl, lv)

    p.wait("sync", "DVT", None)
    units = [(hp, pn) for hp in range(NH1 // 2) for pn in range(NTT)]
    pend = None
    for ui, (hp, pn) in enumerate(units):
        half = ui % 2
        vsl = ui % 2
        load_vst(hp, pn, vsl)
        out = {"lrec": {}, "dgcol": {}}
        chunks = prep_chunks(hp, pn, half, out)
        if pend is None:
            for c_ in chunks:
                c_()
        else:
            seq_panel(*(pend + (chunks,)))
        vstate = chains_panel(hp, pn, half, out["lrec"])
        pend = (hp, pn, half, out["lrec"], out["dgcol"], vstate, vsl)
    seq_panel(*(pend + ([],)))
    p.setlast(("wvsec",), "T", p.v("T"))

    p.wait("tensor", "V", p.v("V"))
    p.wait("tensor", "A", p.v("A"))
    for i in range(16):
        for dc4 in range(NC // 4):
            p.guard("tensor", ("PTB",))
            tp = None
            for lc in range(4):
                dc = dc4 * 4 + lc
                tp = tpose(p, PTB[:, lc * 128:(lc + 1) * 128],
                           am1[:, i * 1024 + dc * 128: i * 1024 + dc * 128 + 128], idt[:])
            p.wait("vector", "T", tp)
            for lc in range(4):
                dc = dc4 * 4 + lc
                ev = vcopy(p, ch(amT, dc, i * 128, i * 128 + 128),
                           PTB[:, lc * 128:(lc + 1) * 128])
                p.setlast(("amT", dc), "V", ev)
            p.setlast(("PTB",), "V", ev)

    xtmp = ctx.sbuf("xtmp", [128, 2 * T], F32)
    x1i = ctx.dram[src_key]
    x1o = ctx.dram[dst_key]
    for dc in range(NC):
        p.wait("tensor", *p.last[("amT", dc)])
    p.guard("sync", ("wvsec",))
    p.wait("sync", "V", p.v("V"))
    for co in range(NC):
        sl = co % 2
        p.guard("gpsimd", ("wqb", sl))
        g = dma(p, "gpsimd", wqb[:, sl * 1024:(sl + 1) * 1024].rearrange("p (cc o) -> p cc o", o=128),
                wd_r[:, :, co * 128:(co + 1) * 128], inc="DG%d" % sl)
        p.wait("tensor", "DG%d" % sl, g)
        p.guard("sync", ("xtmp", sl))
        xl = dma(p, "sync", xtmp[:, sl * T:(sl + 1) * T], x1i[co * 128:(co + 1) * 128, :],
                 inc="DS%d" % sl)
        p.wait("vector", "DS%d" % sl, xl)
        tm = None
        for tt in range(NTT):
            bi = 4 + (tt % 2)
            p.guard("tensor", ("bank", bi))
            for c in range(NC):
                tm = mm(p, bank(bi), wqb[:, sl * 1024 + c * 128: sl * 1024 + c * 128 + 128],
                        ch(amT, c, tt * 512, tt * 512 + 512), start=(c == 0), stop=(c == NC - 1))
            p.wait("vector", "T", tm)
            ev = vstt(p, xtmp[:, sl * T + tt * 512: sl * T + tt * 512 + 512], bank(bi),
                      bdc[:, co:co + 1], xtmp[:, sl * T + tt * 512: sl * T + tt * 512 + 512],
                      ALU.add, ALU.add)
            p.setlast(("bank", bi), "V", ev)
        p.setlast(("wqb", sl), "T", tm)
        p.wait("sync", "V", ev)
        st = dma(p, "sync", x1o[co * 128:(co + 1) * 128, :], xtmp[:, sl * T:(sl + 1) * T],
                 inc="DT%d" % sl)
        p.setlast(("xtmp", sl), "DT%d" % sl, st)
        p.setlast(("x1st", co), "DT%d" % sl, st)
    return None



def emit_ffn1(ctx, xn, xn_ready, src_key="x1i", dst_key="y"):
    p = ctx.p
    bank = ctx.bank
    xr = ctx.sbuf("q1", [128, NC * T], BF16)
    xk = ctx.sbuf("k1", [128, NC * T], BF16)
    rr = ctx.sbuf("amT", [128, NC * T], BF16)
    kact = ctx.sbuf("am1b", [128, NC * T], BF16)
    xsh = ctx.sbuf("xsh", [128, T], BF16)
    kraw = ctx.sbuf("kraw", [128, 512], BF16)
    wqb = ctx.sbuf("wqb", [128, 2 * 1024], BF16)
    muc = ctx.sbuf("muc", [128, 16], F32)
    omuc = ctx.sbuf("omuc", [128, 16], F32)
    wr_r = ctx.dram["wrT"].rearrange("(cc p) o -> p cc o", p=128)
    wk_r = ctx.dram["wkT"].rearrange("(cc p) o -> p cc o", p=128)
    wv_r = ctx.dram["wvT"].rearrange("(cc p) o -> p cc o", p=128)

    dma(p, "sync", muc[:, 0:8], ctx.dram["murc"], inc=ctx.DSL)
    dma(p, "sync", muc[:, 8:16], ctx.dram["mukc"], inc=ctx.DSL)
    p.wait("vector", ctx.DSL, None)
    om = p.emit("vector", lambda e, o=omuc[:], i=muc[:]: e.tensor_scalar(o, i, -1.0, 1.0, ALU.mult, ALU.add),
                inc="V")

    mix_done = [0] * NC
    for c in range(NC):
        wait_all(p, "vector", [xn_ready[c]])
        p.guard("vector", ("xsh",))
        p.guard("gpsimd", ("xsh",))
        z = gmemset(p, xsh[:, 0:1], 0.0, inc="G")
        sh = vcopy(p, xsh[:, 1:T], ch(xn, c, 0, T - 1))
        p.wait("vector", "G", z)
        p.wait("vector", "V", om)
        a1 = vts(p, ch(xr, c), ch(xn, c), muc[:, c:c + 1], ALU.mult)
        a2 = vstt(p, ch(xr, c), xsh[:], omuc[:, c:c + 1], ch(xr, c), ALU.mult, ALU.add)
        b1 = vts(p, ch(xk, c), ch(xn, c), muc[:, 8 + c:8 + c + 1], ALU.mult)
        b2 = vstt(p, ch(xk, c), xsh[:], omuc[:, 8 + c:8 + c + 1], ch(xk, c), ALU.mult, ALU.add)
        p.setlast(("xsh",), "V", b2)
        mix_done[c] = b2

    for c in range(NC):
        p.wait("tensor", "V", mix_done[c])
    units = []
    for co in range(NC):
        units.append((wr_r, xr, rr, "sig", co))
        units.append((wk_r, xk, kact, "sq", co))
    for ui, (wsrc, src, dst, mode, co) in enumerate(units):
        sl = ui % 2
        p.guard("gpsimd", ("wqb", sl))
        g = dma(p, "gpsimd", wqb[:, sl * 1024:(sl + 1) * 1024].rearrange("p (cc o) -> p cc o", o=128),
                wsrc[:, :, co * 128:(co + 1) * 128], inc="DG%d" % sl)
        p.wait("tensor", "DG%d" % sl, g)
        tm = None
        for tt in range(NTT):
            bi = tt % 4
            p.guard("tensor", ("bank", bi))
            for c in range(NC):
                tm = mm(p, bank(bi), wqb[:, sl * 1024 + c * 128: sl * 1024 + c * 128 + 128],
                        ch(src, c, tt * 512, tt * 512 + 512), start=(c == 0), stop=(c == NC - 1))
            p.wait("scalar", "T", tm)
            if mode == "sig":
                ev = act(p, ch(dst, co, tt * 512, tt * 512 + 512), bank(bi), AF.Sigmoid)
                p.setlast(("bank", bi), "A", ev)
            else:
                p.guard("scalar", ("kraw",))
                ev = acopy(p, kraw[:], bank(bi))
                p.setlast(("bank", bi), "A", ev)
                p.wait("vector", "A", ev)
                kv = vstt(p, ch(dst, co, tt * 512, tt * 512 + 512), kraw[:], 0.0, kraw[:],
                          ALU.max, ALU.mult)
                p.setlast(("kraw",), "V", kv)
        p.setlast(("wqb", sl), "T", tm)
    kact_done = p.v("V")
    rr_done = p.v("A")

    xtmp = ctx.sbuf("xtmp", [128, 2 * T], F32)
    x1i = ctx.dram[src_key]
    yo = ctx.dram[dst_key]
    p.wait("tensor", "V", kact_done)
    for co in range(NC):
        sl = co % 2
        p.guard("gpsimd", ("wqb", sl))
        g = dma(p, "gpsimd", wqb[:, sl * 1024:(sl + 1) * 1024].rearrange("p (cc o) -> p cc o", o=128),
                wv_r[:, :, co * 128:(co + 1) * 128], inc="DG%d" % sl)
        p.wait("tensor", "DG%d" % sl, g)
        p.guard("sync", ("xtmp", sl))
        xl = dma(p, "sync", xtmp[:, sl * T:(sl + 1) * T], x1i[co * 128:(co + 1) * 128, :],
                 inc="DS%d" % sl)
        p.wait("vector", "DS%d" % sl, xl)
        p.wait("vector", "A", rr_done)
        tm = None
        for tt in range(NTT):
            bi = tt % 4
            p.guard("tensor", ("bank", bi))
            for c in range(NC):
                tm = mm(p, bank(bi), wqb[:, sl * 1024 + c * 128: sl * 1024 + c * 128 + 128],
                        ch(kact, c, tt * 512, tt * 512 + 512), start=(c == 0), stop=(c == NC - 1))
            p.wait("vector", "T", tm)
            p.guard("vector", ("tmpf",))
            t1 = vtt(p, ctx.sb["tmpf"][:], bank(bi), ch(rr, co, tt * 512, tt * 512 + 512), ALU.mult)
            p.setlast(("bank", bi), "V", t1)
            ev = vtt(p, xtmp[:, sl * T + tt * 512: sl * T + tt * 512 + 512],
                     xtmp[:, sl * T + tt * 512: sl * T + tt * 512 + 512], ctx.sb["tmpf"][:], ALU.add)
            p.setlast(("tmpf",), "V", ev)
        p.setlast(("wqb", sl), "T", tm)
        p.wait("sync", "V", ev)
        st = dma(p, "sync", yo[co * 128:(co + 1) * 128, :], xtmp[:, sl * T:(sl + 1) * T],
                 inc="DT%d" % sl)
        p.setlast(("xtmp", sl), "DT%d" % sl, st)
    return None



def emit_load_f32(ctx, tile, name):
    p = ctx.p
    src = ctx.dram[name]
    for c in range(NC):
        p.guard("sync", ("x1", c))
        dma(p, "sync", ch(tile, c), src[c * 128:(c + 1) * 128, :], inc=ctx.DSL)
    for c in range(NC):
        p.setlast(("x1", c), ctx.DSL, None)
    return [(ctx.DSL, None)] * NC


def emit_load_bf16(ctx, tile, name):
    p = ctx.p
    src = ctx.dram[name]
    for c in range(NC):
        dma(p, "sync", ch(tile, c), src[c * 128:(c + 1) * 128, :], inc=ctx.DSL)
    return [[(ctx.DSL, None)] for _ in range(NC)]


def emit_store(ctx, tile, ready, name):
    p = ctx.p
    dst = ctx.dram[name]
    last = None
    for c in range(NC):
        wait_all(p, "sync", [ready[c]] if isinstance(ready[c], tuple) else ready[c])
        last = dma(p, "sync", dst[c * 128:(c + 1) * 128, :], ch(tile, c), inc="DST")
    return last


def emit_resadd(ctx, x1, x1_ready, name):
    p = ctx.p
    tmpd = ctx.sbuf("tmpd", [128, 2 * T], F32)
    src = ctx.dram[name]
    out = []
    for c in range(NC):
        sl = c % 2
        p.guard("sync", ("tmpd", sl))
        m = dma(p, "sync", tmpd[:, sl * T:(sl + 1) * T], src[c * 128:(c + 1) * 128, :], inc="DS%d" % sl)
        p.wait("vector", "DS%d" % sl, m)
        wait_all(p, "vector", [x1_ready[c]] if isinstance(x1_ready[c], tuple) else x1_ready[c])
        ev = vtt(p, ch(x1, c), ch(x1, c), tmpd[:, sl * T:(sl + 1) * T], ALU.add)
        p.setlast(("tmpd", sl), "V", ev)
        p.setlast(("x1", c), "V", ev)
        out.append([("V", ev)])
    return out



def _finish(ctx):
    p = ctx.p
    for s in [n for n in list(p.val.keys()) if n.startswith("D")]:
        if p.v(s):
            p.wait("sync", s, p.v(s))
    ctx.p.finalize()


def phase_barrier(ctx, skip=()):
    p = ctx.p
    snap = {n: p.val[n] for n in list(p.val.keys()) if p.val[n] > 0}
    own = {"tensor": "T", "vector": "V", "scalar": "A", "gpsimd": "G"}
    for eng in ENGINES:
        if eng in skip:
            continue
        for n, v in snap.items():
            if own.get(eng) == n:
                continue
            p.wait(eng, n, v)


PHASE_WEIGHTS = {
    "A": [],
    "B": ["w1T", "w2T", "h1T", "h2T", "w3T", "s1c", "s2c", "sh1c", "sh2c", "s3c"],
    "C": ["wqkv0", "wout0", "hs0", "D3", "I3"],
    "D": [],
    "E": ["wqkv1", "bqkv1c", "bqkv1v", "wd1", "bd1c", "RT", "cosT", "sinT", "identb"],
    "F": ["wrT", "wkT", "wvT", "murc", "mukc"],
}

PHASE_IO = {
    "A": {"xT": ([C, T], F32, "ExternalInput"), "xn0": ([C, T], BF16, "ExternalOutput")},
    "B": {"xn0": ([C, T], BF16, "ExternalInput"), "ffn0": ([C, T], F32, "ExternalOutput")},
    "C": {"xn0": ([C, T], BF16, "ExternalInput"), "xT": ([C, T], F32, "ExternalInput"),
          "x1o": ([C, T], F32, "ExternalOutput")},
    "D": {"x1i": ([C, T], F32, "ExternalInput"), "ffn0": ([C, T], F32, "ExternalInput"),
          "x1o": ([C, T], F32, "ExternalOutput"), "xn1": ([C, T], BF16, "ExternalOutput")},
    "E": {"x1i": ([C, T], F32, "ExternalInput"), "xn1": ([C, T], BF16, "ExternalInput"),
          "x1o": ([C, T], F32, "ExternalOutput")},
    "F": {"x1i": ([C, T], F32, "ExternalInput"), "xn1": ([C, T], BF16, "ExternalInput"),
          "y": ([C, T], F32, "ExternalOutput")},
}


def build_phase(phase):
    nc = bass.Bass("TRN2", target_bir_lowering=False)
    es = ExitStack()
    with es:
        ctx = Ctx(nc, es, PHASE_WEIGHTS[phase], PHASE_IO[phase])
        ctx.commons()
        p = ctx.p
        if phase == "A":
            x1 = ctx.sbuf("x1", [128, NC * T], F32)
            xn = ctx.sbuf("xn", [128, NC * T], BF16)
            ctx.sbuf("sq", [128, NC * T], BF16)
            ctx.sbuf("rowbuf", [1, 4 * T], F32)
            ctx.sbuf("tmpf", [128, 512], F32)
            for c in range(NC):
                dma(p, "sync", ch(x1, c),
                    ctx.dram["xT"][c * 128:(c + 1) * 128, :], inc=ctx.DSL)
            ready = [[(ctx.DSL, None)] for _ in range(NC)]
            xn_ready = emit_ln(ctx, x1, xn, ready)
            emit_store(ctx, xn, xn_ready, "xn0")
        elif phase == "B":
            xn = ctx.sbuf("xn", [128, NC * T], BF16)
            xn_ready = emit_load_bf16(ctx, xn, "xn0")
            xn_ready = [r[0] for r in xn_ready]
            emit_ffn0(ctx, xn, xn_ready)
        elif phase == "C":
            xn = ctx.sbuf("xn", [128, NC * T], BF16)
            ctx.sbuf("rowbuf", [1, T + 512], F32)
            xn_ready = emit_load_bf16(ctx, xn, "xn0")
            xn_ready = [r[0] for r in xn_ready]
            emit_attn0(ctx, xn, xn_ready)
        elif phase == "D":
            x1 = ctx.sbuf("x1", [128, NC * T], F32)
            xn = ctx.sbuf("xn", [128, NC * T], BF16)
            ctx.sbuf("sq", [128, NC * T], BF16)
            ctx.sbuf("rowbuf", [1, 4 * T], F32)
            ctx.sbuf("tmpf", [128, 512], F32)
            x1_ready = emit_load_f32(ctx, x1, "x1i")
            rdy = emit_resadd(ctx, x1, x1_ready, "ffn0")
            xn_ready = emit_ln(ctx, x1, xn, rdy)
            emit_store(ctx, x1, rdy, "x1o")
            emit_store(ctx, xn, xn_ready, "xn1")
        elif phase == "E":
            xn = ctx.sbuf("xn", [128, NC * T], BF16)
            ctx.sbuf("rowbuf", [1, T + 512], F32)
            xn_ready = emit_load_bf16(ctx, xn, "xn1")
            xn_ready = [r[0] for r in xn_ready]
            emit_attn1(ctx, xn, xn_ready)
        elif phase == "F":
            xn = ctx.sbuf("xn", [128, NC * T], BF16)
            ctx.sbuf("tmpf", [128, 512], F32)
            xn_ready = emit_load_bf16(ctx, xn, "xn1")
            xn_ready = [r[0] for r in xn_ready]
            emit_ffn1(ctx, xn, xn_ready)
        _finish(ctx)
    return nc



_PHASE_CACHE = {}


def _get_phase(phase):
    if phase not in _PHASE_CACHE:
        _PHASE_CACHE[phase] = build_phase(phase)
    return _PHASE_CACHE[phase]


def kernel_unfused_legacy(**inputs):
    h = host_prep(inputs)
    ncores = 8
    cores = list(range(ncores))
    state = [dict() for _ in range(ncores)]

    def run(phase, in_names, weight_names, out_names):
        nc = _get_phase(phase)
        in_maps = []
        for b in range(ncores):
            m = {}
            for n in in_names:
                m[n] = state[b][n]
            for n in weight_names:
                m[n] = h[n]
            in_maps.append(m)
        res = run_bass_kernel_spmd(nc, in_maps, core_ids=cores)
        for b in range(ncores):
            for n in out_names:
                state[b][n] = np.asarray(res.results[b][n])

    for b in range(ncores):
        state[b]["xT"] = h["x_percore"][b]
    run("A", ["xT"], [], ["xn0"])
    run("B", ["xn0"], PHASE_WEIGHTS["B"], ["ffn0"])
    run("C", ["xn0", "xT"], PHASE_WEIGHTS["C"], ["x1o"])
    for b in range(ncores):
        state[b]["x1i"] = state[b].pop("x1o")
    run("D", ["x1i", "ffn0"], [], ["x1o", "xn1"])
    for b in range(ncores):
        state[b]["x1i"] = state[b].pop("x1o")
    run("E", ["x1i", "xn1"], PHASE_WEIGHTS["E"], ["x1o"])
    for b in range(ncores):
        state[b]["x1i"] = state[b].pop("x1o")
    run("F", ["x1i", "xn1"], PHASE_WEIGHTS["F"], ["y"])

    out = np.stack([np.ascontiguousarray(state[b]["y"].T) for b in range(ncores)], axis=0)
    return out.astype(np.float32)



FUSED_WEIGHTS = (PHASE_WEIGHTS["B"] + PHASE_WEIGHTS["C"] + PHASE_WEIGHTS["E"]
                 + PHASE_WEIGHTS["F"])

FUSED_IO = {
    "xT": ([C, T], F32, "ExternalInput"),
    "y": ([C, T], F32, "ExternalOutput"),
    "ffn0": ([C, T], F32, "Internal"),
    "x1b": ([C, T], F32, "Internal"),
    "x1c": ([C, T], F32, "Internal"),
    "vT1d": ([16 * 128, 1024], BF16, "Internal"),
}


def build_fused():
    nc = bass.Bass("TRN2", target_bir_lowering=False)
    es = ExitStack()
    with es:
        ctx = Ctx(nc, es, FUSED_WEIGHTS, FUSED_IO)
        ctx.commons()
        p = ctx.p
        xn = ctx.sbuf("xn", [128, NC * T], BF16)
        ctx.sbuf("wqb", [128, 2 * 1024], BF16)
        ctx.sbuf("stmp", [128, 2 * 512], BF16)
        ctx.sbuf("idt", [128, 128], BF16)
        ctx.sbuf("rt", [128, 128], BF16)
        R1 = ctx.sbuf("R1", [128, 65536], mybir.dt.uint8)
        R2 = ctx.sbuf("R2", [128, 65536], mybir.dt.uint8)
        R3 = ctx.sbuf("R3", [128, 32768], mybir.dt.uint8)

        def place(backing, name, shape, dt, byte_off):
            h = nc.virtual_sbuf_tensor(name, shape, dt)
            nc._tensor(h.name, h.shape, h.dtype, type="SB")
            mloc = nc.lookup_mloc(h)
            mloc.allocated = True
            base = nc.lookup_mloc(backing).addr
            nbytes = shape[1] * mybir.dt.size(dt)
            assert byte_off + nbytes <= backing.shape[1], (name, byte_off, nbytes)
            mloc.addr = base + byte_off
            ctx.sb[name] = h
            return h

        ctx.DSL = "DLA"
        srcA = place(R1, "srcA", [128, 16384], F32, 0)
        place(R3, "sq", [128, 16384], BF16, 0)
        place(R2, "rowbuf", [1, 8192], F32, 0)
        place(R2, "tmpf", [128, 512], F32, 32768)
        for c in range(NC):
            dma(p, "sync", ch(srcA, c), ctx.dram["xT"][c * 128:(c + 1) * 128, :], inc=ctx.DSL)
        ready = [[(ctx.DSL, None)] for _ in range(NC)]
        xn_ready = emit_ln(ctx, srcA, xn, ready)
        phase_barrier(ctx)

        ctx.DSL = "DLB"
        place(R1, "ha", [128, 32 * TF], BF16, 0)
        place(R2, "hb", [128, 32 * TF], BF16, 0)
        place(R3, "wbh", [128, 2 * 4096], BF16, 0)
        place(R3, "wb1", [128, 2 * 1024], BF16, 16384)
        place(R3, "wb2", [128, 2 * 1024], BF16, 20480)
        place(R3, "outst", [128, 2 * TF], F32, 24576)
        emit_ffn0(ctx, xn, xn_ready)
        phase_barrier(ctx)

        ctx.DSL = "DLC"
        place(R1, "q0", [128, NC * T], BF16, 0)
        place(R1, "k0", [128, NC * T], BF16, 32768)
        place(R2, "vT0", [128, 16 * 1024], BF16, 0)
        place(R2, "PT", [128, 16 * 512], BF16, 32768)
        place(R2, "xtmp", [128, 2 * T], F32, 49152)
        place(R2, "wvsec", [128, NC * 512], BF16, 49152)
        place(R3, "sqc", [128, 2 * T], BF16, 0)
        place(R3, "rowbuf", [1, T + 512], F32, 22528)
        place(R3, "rlb", [128, 512], F32, 20480)
        place(R1, "tmpd", [128, 2 * T], F32, 0)
        emit_attn0(ctx, xn, xn_ready, src_key="xT", dst_key="x1b")
        phase_barrier(ctx, skip=("sync",))

        ctx.DSL = "DLD"
        place(R2, "rowbuf", [1, 8192], F32, 0)
        place(R2, "tmpf", [128, 512], F32, 32768)
        place(R2, "xtmp", [128, 2 * T], F32, 49152)
        place(R3, "sq", [128, 16384], BF16, 0)
        xn_ready = emit_ln_stream(ctx, xn, ctx.sb["sq"], "x1b")
        phase_barrier(ctx)

        ctx.DSL = "DLF"
        place(R1, "q1", [128, NC * T], BF16, 0)
        place(R1, "k1", [128, NC * T], BF16, 32768)
        place(R2, "amT", [128, NC * T], BF16, 0)
        place(R2, "am1b", [128, NC * T], BF16, 32768)
        place(R3, "xtmp", [128, 2 * T], F32, 0)
        place(R3, "xsh", [128, T], BF16, 16384)
        place(R3, "kraw", [128, 512], BF16, 20480)
        place(R3, "muc", [128, 16], F32, 21504)
        place(R3, "omuc", [128, 16], F32, 21568)
        place(R3, "tmpf", [128, 512], F32, 22528)
        emit_ffn1(ctx, xn, xn_ready, src_key="x1b", dst_key="x1c")
        phase_barrier(ctx)

        ctx.DSL = "DLE"
        place(R1, "q1", [128, NC * T], BF16, 0)
        place(R1, "k1", [128, NC * T], BF16, 32768)
        place(R2, "vT1", [128, 16 * 1024], BF16, 0)
        place(R2, "PT", [128, 16 * 512], BF16, 32768)
        place(R2, "ET2", [128, 16 * 512], BF16, 49152)
        place(R2, "xtmp", [128, 2 * T], F32, 49152)
        place(R3, "CHst", [128, 40 * 128], BF16, 0)
        place(R3, "Ab", [128, 16 * 80], BF16, 10240)
        place(R3, "Ab2", [128, 16 * 80], BF16, 12800)
        place(R3, "yb", [128, 2 * 64], BF16, 15360)
        place(R3, "yb2", [128, 2 * 64], BF16, 15616)
        place(R3, "dgt", [128, 128], BF16, 15872)
        place(R3, "bqc", [128, 24], F32, 16128)
        place(R3, "bqs", [128, 8], F32, 16224)
        place(R3, "bdc", [128, 8], F32, 16256)
        place(R3, "tmb", [128, 2 * 512], BF16, 16288)
        place(R3, "bvb", [1, C], BF16, 16288)
        place(R3, "orb", [1, 128], BF16, 18336)
        place(R3, "wvsec", [128, NC * 512], BF16, 18592)
        place(R3, "rowbuf", [1, T], F32, 18592)
        place(R3, "vst", [128, 2 * 512], BF16, 26784)
        emit_attn1(ctx, xn, xn_ready, src_key="x1c", dst_key="y")
        _finish(ctx)
    return nc


_FUSED_CACHE = []


def kernel_fused(inputs):
    h = host_prep(inputs)
    if not _FUSED_CACHE:
        _FUSED_CACHE.append(build_fused())
    nc = _FUSED_CACHE[0]
    in_maps = []
    for b in range(8):
        m = {"xT": h["x_percore"][b]}
        for n in FUSED_WEIGHTS:
            m[n] = h[n]
        in_maps.append(m)
    res = run_bass_kernel_spmd(nc, in_maps, core_ids=list(range(8)))
    out = np.stack([np.ascontiguousarray(np.asarray(res.results[b]["y"]).T) for b in range(8)], axis=0)
    return out.astype(np.float32)


def kernel(**inputs):
    return kernel_fused(inputs)
